# version 12
# baseline (speedup 1.0000x reference)
import contextlib
import os

import numpy as np
import ml_dtypes

import concourse.bass as bass
import concourse.mybir as mybir
from concourse.bass_utils import run_bass_kernel_spmd

F32 = mybir.dt.float32
BF16 = mybir.dt.bfloat16
U8 = mybir.dt.uint8
AF = mybir.ActivationFunctionType
ALU = mybir.AluOpType
AX = mybir.AxisListType

S = 2048
D = 2048
NCH = 16
NH = 32
HD = 64
DKV = 512
EPS = 1e-5

DEBUG = bool(int(os.environ.get("MK_DEBUG", "0")))
MAXPH = int(os.environ.get("MK_MAXPH", "99"))
F0CUT = int(os.environ.get("MK_F0CUT", "9"))
FCUT = int(os.environ.get("MK_FCUT", "9"))


def host_tables():
    inv = 10000.0 ** (-np.arange(0, HD, 2, dtype=np.float64) / HD)
    pos = np.arange(S, dtype=np.float64)
    ang = pos[None, :] * inv[:, None]
    p = np.arange(128)
    cos_t = np.cos(ang)[p % 32, :]
    sgn = np.where((p % 64) < 32, -1.0, 1.0)[:, None]
    sinsgn_t = sgn * np.sin(ang)[p % 32, :]
    return (cos_t.astype(np.float32).astype(ml_dtypes.bfloat16),
            sinsgn_t.astype(np.float32).astype(ml_dtypes.bfloat16),
            np.eye(128, dtype=np.float32).astype(ml_dtypes.bfloat16))


class Ctr:
    def __init__(self, handle):
        self.h = handle
        self.n = 0

    def inc(self, instr, k=1):
        instr.then_inc(self.h, k)
        self.n += k
        return self.n


class NS:
    pass


def build_nc():
    nc = bass.Bass("TRN2", target_bir_lowering=False)
    Z = NS()
    Z.nc = nc

    Z.x_h = nc.dram_tensor("x", [S, D], F32, kind="ExternalInput").ap()
    Z.wq_h = nc.dram_tensor("wq", [D, D], F32, kind="ExternalInput").ap()
    Z.wk_h = nc.dram_tensor("wk", [D, DKV], F32, kind="ExternalInput").ap()
    Z.wv_h = nc.dram_tensor("wv", [D, DKV], F32, kind="ExternalInput").ap()
    Z.wphi_h = nc.dram_tensor("wphi", [D, D], F32, kind="ExternalInput").ap()
    Z.wo_h = nc.dram_tensor("wo", [D, D], F32, kind="ExternalInput").ap()
    Z.wffn_h = nc.dram_tensor("wffn", [D, D], F32, kind="ExternalInput").ap()
    Z.gamma_h = nc.dram_tensor("gamma", [128, 16], F32, kind="ExternalInput").ap()
    Z.rows3_h = nc.dram_tensor("rows3", [1, 3 * D], F32, kind="ExternalInput").ap()
    Z.cos_h = nc.dram_tensor("cos_t", [128, S], BF16, kind="ExternalInput").ap()
    Z.sin_h = nc.dram_tensor("sinsgn_t", [128, S], BF16, kind="ExternalInput").ap()
    Z.ident_h = nc.dram_tensor("ident", [128, 128], BF16, kind="ExternalInput").ap()
    Z.y_h = nc.dram_tensor("y", [S, D], F32, kind="ExternalOutput").ap()
    spill_kind = "ExternalOutput" if DEBUG else "Internal"
    Z.row_sp = nc.dram_tensor("row_sp", [1, 2048], BF16, kind="Internal").ap()
    Z.wk_bf = nc.dram_tensor("wk_bf", [D, DKV], BF16, kind="Internal").ap()
    Z.wv_bf = nc.dram_tensor("wv_bf", [D, DKV], BF16, kind="Internal").ap()
    Z.wphi_bf = nc.dram_tensor("wphi_bf", [D, D], BF16, kind="Internal").ap()
    Z.wo_bf = nc.dram_tensor("wo_bf", [D, D], BF16, kind="Internal").ap()
    Z.wf_bf = nc.dram_tensor("wf_bf", [D, D], BF16, kind="Internal").ap()
    Z.qk_sp = nc.dram_tensor("qk_sp", [NCH, 128, S], BF16, kind=spill_kind).ap()
    Z.phi_sp = nc.dram_tensor("phi_sp", [NCH, 128, S], BF16, kind=spill_kind).ap()
    Z.ao_sp = nc.dram_tensor("ao_sp", [NCH, 128, D], BF16, kind=spill_kind).ap()
    Z.qg_dram = nc.dram_tensor("qg_dram", [16, 128], BF16, kind=spill_kind).ap()
    Z.dbg = {}
    if DEBUG:
        for nm, sh, dt in [("dbg_xn1", [128, NCH * S], BF16),
                           ("dbg_kkt", [128, NCH * DKV], BF16),
                           ("dbg_kkf", [128, 4 * S], BF16),
                           ("dbg_vt", [128, NCH * DKV], BF16),
                           ("dbg_alpha", [128, 512], BF16),
                           ("dbg_bd", [128, 2048], BF16),
                           ("dbg_ctx", [128, NCH * S], BF16),
                           ("dbg_qwt", [128, NCH * D], BF16),
                           ("dbg_qgarr", [128, 128], BF16),
                           ("dbg_scale", [128, 1], F32)]:
            Z.dbg[nm] = nc.dram_tensor(nm, sh, dt, kind="ExternalOutput").ap()

    es = contextlib.ExitStack()

    def sbt(name, shape, dtype):
        return es.enter_context(nc.sbuf_tensor(name, shape, dtype))

    R0 = sbt("R0", [128, 65536], U8)
    R1 = sbt("R1", [128, 65536], U8)
    R2 = sbt("R2", [128, 32768], U8)
    R3 = sbt("R3", [128, 32768], U8)
    SM1 = sbt("SM1", [128, 4096], U8)
    SM2 = sbt("SM2", [128, 4096], U8)
    Z.ident = sbt("ident_sb", [128, 128], BF16)
    Z.gamma_sb = sbt("gamma_sb", [128, 16], F32)
    Z.qg_sb = sbt("qg_sb", [128, 16], F32)
    Z.qg_bf = sbt("qg_bf", [128, 16], BF16)
    Z.qg_tr = sbt("qg_tr", [16, 128], BF16)
    Z.qg_arr = sbt("qg_arr", [128, 128], BF16)
    Z.m2_sb = sbt("m2_sb", [128, 16], F32)
    Z.r2_sb = sbt("r2_sb", [128, 16], F32)
    Z.bd_sb = sbt("bd_sb", [128, 16 * 128], BF16)
    Z.alpha_tm = sbt("alpha_tm", [128, 512], BF16)
    Z.scale_col = sbt("scale_col", [128, 1], F32)
    Z.scale_bf = sbt("scale_bf", [1, 1], BF16)
    Z.stat = sbt("stat", [128, 64], F32)
    Z.maxw_sb = sbt("maxw_sb", [128, 18], F32)
    Z.maxw_bf = sbt("maxw_bf", [128, 1], BF16)
    Z.sm_cols = sbt("sm_cols", [32, 4], F32)
    Z.coll_sb = sbt("coll_sb", [128, 16], F32)
    Z.coll_bf = sbt("coll_bf", [128, 16], BF16)
    Z.rowl_ev = sbt("rowl_ev", [16, 128], BF16)
    Z.consts = sbt("consts", [128, 8], F32)
    nc.const_aps.register(F32, float(EPS), Z.consts[:, 0:1])
    nc.const_aps.register(F32, -0.5, Z.consts[:, 1:2])
    nc.const_aps.register(F32, float(1.0 + EPS), Z.consts[:, 2:3])
    nc.const_aps.register(F32, float(EPS * EPS), Z.consts[:, 3:4])
    nc.const_aps.register(F32, 0.0, Z.consts[:, 4:5])
    Z.ones_bf = sbt("ones_bf", [1, 128], BF16)

    def carve(region, off, n, dtype):
        esz = 2 if dtype == BF16 else 4
        assert off + n * esz <= region.shape[1], (off, n, dtype)
        return region[:, off:off + n * esz].bitcast(dtype)

    Z.xn1 = carve(R0, 0, 32768, BF16)
    Z.qkroll = [carve(R0, i * 4096, 2048, BF16) for i in range(2)]
    Z.phiroll = [carve(R0, 8192 + i * 4096, 2048, BF16) for i in range(2)]
    Z.ra_ev = [carve(R0, 16384 + i * 4096, 2048, BF16) for i in range(2)]
    Z.ao_ev = [carve(R0, 24576 + i * 4096, 2048, BF16) for i in range(2)]
    Z.qwt = carve(R0, 0, 32768, BF16)
    Z.slot = [carve(R1, i * 4096, 2048, BF16) for i in range(5)]
    Z.slot2 = [carve(R1, 20480 + i * 4096, 2048, BF16) for i in range(5)]
    Z.qkspl = [carve(R1, 40960 + i * 4096, 2048, BF16) for i in range(2)]
    Z.xroll = [carve(R1, 49152 + i * 8192, 2048, F32) for i in range(2)]
    Z.vt = carve(R1, 0, 8192, BF16)
    Z.kkt = carve(R1, 16384, 8192, BF16)
    Z.phspl = [carve(R1, 32768 + i * 4096, 2048, BF16) for i in range(2)]
    Z.kkf = carve(R1, 40960, 8192, BF16)
    Z.wks = [carve(R1, 40960 + i * 4096, 2048, BF16) for i in range(6)]
    Z.ctx = carve(R1, 0, 32768, BF16)
    Z.xn2w = [carve(R1, i * 4096, 2048, BF16) for i in range(2)]
    Z.g_x2 = [carve(R1, 8192 + i * 8192, 2048, F32) for i in range(2)]
    Z.g_t = [carve(R1, 24576 + i * 4096, 2048, BF16) for i in range(2)]
    Z.f_x1 = [carve(R1, 32768 + i * 8192, 2048, F32) for i in range(3)]
    Z.f_t = [carve(R1, 57344, 2048, F32)]
    Z.WA = carve(R2, 0, 16384, BF16)
    Z.WB = carve(R3, 0, 16384, BF16)
    Z.exp_row = carve(SM1, 0, 2048, BF16)
    Z.alpha_bf = carve(SM2, 0, 2048, BF16)
    Z.wf_roll = [carve(R3, i * 4096, 2048, BF16) for i in range(2)]
    Z.wf_sgn = [carve(R3, 16384 + i * 4096, 2048, BF16) for i in range(2)]
    Z.wf_sg2 = [carve(R3, 24576 + i * 4096, 2048, BF16) for i in range(2)]
    Z.rows_f32 = carve(R2, 8192, 3 * D, F32)[0:1, :]
    Z.rows_bf = carve(R2, 0, 3 * D, BF16)[0:1, :]
    Z.lnw_bc = carve(R1, 57344, 2048, BF16)
    Z.lnb_bc = carve(R1, 61440, 2048, BF16)
    Z.f0_junk = [carve(R1, 49152 + i * 4096, 2048, BF16) for i in range(2)]
    Z.f_x = [carve(R2, i * 8192, 2048, F32) for i in range(2)]
    Z.bt_row = carve(R2, 24576, 2048, BF16)[0:1, :]
    Z.bt_aux = carve(R2, 28672, 2048, BF16)[0:1, :]
    Z.bt_tmp = Z.f0_junk[0][0:1, :]
    Z.f_ao = [carve(R3, i * 4096, 2048, BF16) for i in range(2)]
    Z.f_u = carve(R3, 8192, 2048, BF16)
    Z.f_junk = carve(R3, 12288, 2048, BF16)
    Z.f_stg = [carve(R3, 16384 + i * 4096, 2048, BF16) for i in range(2)]
    Z.cos_t = carve(SM1, 0, 2048, BF16)
    Z.sinsgn_t = carve(SM2, 0, 2048, BF16)

    Z.psA = es.enter_context(nc.psum_tensor("psA", [128, 2048], F32))
    Z.psB = es.enter_context(nc.psum_tensor("psB", [128, 2048], F32))

    sem_list = ["pre", "cst", "dbg", "wq0", "wq1",
                "wc_kv", "wc_ph", "wc_wo", "wc_wf",
                "a_x0", "a_x1", "a_s", "a_v", "a_pe", "a_ev",
                "b_pe", "b_s", "b_v", "b_sp0", "b_sp1",
                "kv_w", "k_pe", "k_s", "k_v", "v_pe", "v_ev",
                "pw0", "pw1", "p_pe", "p_s", "p_sp0", "p_sp1",
                "c_qg", "c_pe", "c_v", "c_s", "c_dma",
                "d_qk0", "d_qk1", "d_ph0", "d_ph1", "d_pe", "d_v", "wo_w",
                "e_pe", "e_s", "e_sp0", "e_sp1",
                "f0_r", "f0_r2", "f0_w0", "f0_w1", "f0_s", "f0_e", "f0_v", "f0_pe",
                "f_x0", "f_x1", "f_ao0", "f_ao1", "f_v", "f_s", "f_pe",
                "f_p", "f_out0", "f_out1"]
    Z.C = {n: Ctr(es.enter_context(nc.semaphore(n))) for n in sem_list}

    Z.block = es.enter_context(nc.Block())
    with es:
        _emit(Z)
    return nc


def _emit(Z):
    block = Z.block
    C = Z.C
    psA, psB = Z.psA, Z.psB

    @block.sync
    def _(sync):
        C["pre"].inc(sync.dma_start(out=Z.ident[:], in_=Z.ident_h), 16)
        C["pre"].inc(sync.dma_start(out=Z.gamma_sb[:], in_=Z.gamma_h), 16)
        C["pre"].inc(sync.dma_start(out=Z.cos_t, in_=Z.cos_h), 16)
        C["pre"].inc(sync.dma_start(out=Z.sinsgn_t, in_=Z.sin_h), 16)
    PRE = C["pre"].n

    @block.gpsimd
    def _(gp):
        for dst, h, sem in ((Z.WA, 0, "wq0"), (Z.WB, 1, "wq1")):
            for k in range(NCH):
                C[sem].inc(gp.dma_start(
                    out=dst[:, k * 1024:(k + 1) * 1024],
                    in_=Z.wq_h[k * 128:(k + 1) * 128, h * 1024:(h + 1) * 1024]), 16)
        for src, dst, cs in ((Z.wk_h, Z.wk_bf, "wc_kv"), (Z.wv_h, Z.wv_bf, "wc_kv"),
                             (Z.wphi_h, Z.wphi_bf, "wc_ph"), (Z.wo_h, Z.wo_bf, "wc_wo"),
                             (Z.wffn_h, Z.wf_bf, "wc_wf")):
            for k in range(NCH):
                C[cs].inc(gp.dma_start(out=dst[k * 128:(k + 1) * 128, :],
                                       in_=src[k * 128:(k + 1) * 128, :]), 16)
    WQH = 16 * 16

    a_x_at = []

    @block.sync
    def _(sync):
        for c in range(NCH):
            if c >= 2:
                sync.wait_ge(C["a_v"].h, 2 * (c - 2) + 2)
            a_x_at.append(C["a_x0" if c % 2 == 0 else "a_x1"].inc(sync.dma_start(
                out=Z.xroll[c % 2][:], in_=Z.x_h[c * 128:(c + 1) * 128, :]), 16))

    @block.scalar
    def _(s):
        st = Z.stat
        s.wait_ge(C["cst"].h, 1)
        for c in range(NCH):
            b = 32 * (c % 2)
            s.wait_ge(C["a_x0" if c % 2 == 0 else "a_x1"].h, a_x_at[c])
            C["a_s"].inc(s.activation(Z.slot[c % 2], Z.xroll[c % 2][:], AF.Square,
                                      accum_out=st[:, b + 0:b + 1]))
            s.wait_ge(C["a_v"].h, 2 * c + 1)
            s.activation(st[:, b + 3:b + 4], st[:, b + 6:b + 7], AF.Ln,
                         bias=float(EPS))
            s.drain()
            C["a_s"].inc(s.activation(st[:, b + 4:b + 5], st[:, b + 3:b + 4],
                                      AF.Exp, scale=-0.5))

    @block.vector
    def _(v):
        st = Z.stat
        v.memset(Z.consts[:, 0:1], float(EPS))
        v.memset(Z.consts[:, 1:2], -0.5)
        v.memset(Z.consts[:, 2:3], float(1.0 + EPS))
        v.memset(Z.consts[:, 3:4], float(EPS * EPS))
        C["cst"].inc(v.memset(Z.consts[:, 4:5], 0.0))
        for c in range(NCH):
            b = 32 * (c % 2)
            v.wait_ge(C["a_x0" if c % 2 == 0 else "a_x1"].h, a_x_at[c])
            v.reduce_sum(st[:, b + 7:b + 8], Z.xroll[c % 2][:], axis=AX.X)
            v.drain()
            v.tensor_scalar_mul(st[:, b + 1:b + 2], st[:, b + 7:b + 8], 1.0 / D)
            v.wait_ge(C["a_s"].h, 2 * c + 1)
            v.tensor_scalar_mul(st[:, b + 2:b + 3], st[:, b + 0:b + 1], 1.0 / D)
            v.drain()
            v.tensor_mul(st[:, b + 5:b + 6], st[:, b + 1:b + 2], st[:, b + 1:b + 2])
            v.drain()
            C["a_v"].inc(v.tensor_sub(st[:, b + 6:b + 7], st[:, b + 2:b + 3],
                                      st[:, b + 5:b + 6]))
            v.wait_ge(C["a_s"].h, 2 * c + 2)
            if c >= 2:
                v.wait_ge(C["a_pe"].h, c - 1)
            C["a_v"].inc(v.tensor_scalar(Z.slot[2 + c % 2], Z.xroll[c % 2][:],
                                         st[:, b + 1:b + 2], st[:, b + 4:b + 5],
                                         ALU.subtract, ALU.mult))
            if c >= 1:
                v.wait_ge(C["a_pe"].h, c)
                grp = psA if (c - 1) % 2 == 0 else psB
                outv = Z.xn1.rearrange("p (a b) -> p a b", b=2048)[
                    :, :, (c - 1) * 128:c * 128]
                inv = grp[:].rearrange("p (a b) -> p a b", b=128)
                C["a_ev"].inc(v.tensor_copy(outv, inv))
        v.wait_ge(C["a_pe"].h, NCH)
        grp = psA if (NCH - 1) % 2 == 0 else psB
        outv = Z.xn1.rearrange("p (a b) -> p a b", b=2048)[
            :, :, (NCH - 1) * 128:NCH * 128]
        inv = grp[:].rearrange("p (a b) -> p a b", b=128)
        C["a_ev"].inc(v.tensor_copy(outv, inv))

    @block.tensor
    def _(t):
        t.wait_ge(C["pre"].h, PRE)
        for c in range(NCH):
            t.wait_ge(C["a_v"].h, 2 * c + 2)
            if c >= 2:
                t.wait_ge(C["a_ev"].h, c - 1)
            grp = psA if c % 2 == 0 else psB
            stg = Z.slot[2 + c % 2]
            for i in range(16):
                mm = t.matmul(grp[:, i * 128:(i + 1) * 128],
                              stg[:, i * 128:(i + 1) * 128], Z.ident[:],
                              start=True, stop=True)
            C["a_pe"].inc(mm)

    if DEBUG:
        @block.sync
        def _(sync):
            sync.wait_ge(C["a_ev"].h, NCH)
            C["dbg"].inc(sync.dma_start(out=Z.dbg["dbg_xn1"], in_=Z.xn1), 16)
    if MAXPH < 2:
        _tail(block, C, [("a_ev", NCH)])
        return

    @block.tensor
    def _(t):
        t.wait_ge(C["a_ev"].h, NCH)
        for oc in range(NCH):
            t.wait_ge(C["wq0" if oc < 8 else "wq1"].h, WQH)
            if oc >= 2:
                t.wait_ge(C["b_s"].h, 2 * (oc - 2) + 1)
            grp = psA if oc % 2 == 0 else psB
            for k in range(NCH):
                if oc < 8:
                    wt = Z.WA[:, k * 1024 + oc * 128:k * 1024 + oc * 128 + 128]
                else:
                    wt = Z.WB[:, k * 1024 + (oc - 8) * 128:k * 1024 + (oc - 8) * 128 + 128]
                for j in range(4):
                    mm = t.matmul(grp[:, j * 512:(j + 1) * 512], wt,
                                  Z.xn1[:, k * 2048 + j * 512:k * 2048 + (j + 1) * 512],
                                  start=(k == 0), stop=(k == NCH - 1))
            C["b_pe"].inc(mm)

    @block.scalar
    def _(s):
        for oc in range(NCH):
            sl = Z.slot if oc % 2 == 0 else Z.slot2
            grp = psA if oc % 2 == 0 else psB
            s.wait_ge(C["b_pe"].h, oc + 1)
            if oc >= 2:
                s.wait_ge(C["b_v"].h, 3 * (oc - 2) + 3)
            C["b_s"].inc(s.activation(sl[0], grp[:], AF.Copy))
            s.wait_ge(C["b_v"].h, 3 * oc + 2)
            C["b_s"].inc(s.activation(sl[2], sl[3], AF.Exp))

    @block.vector
    def _(v):
        for oc in range(NCH):
            sl = Z.slot if oc % 2 == 0 else Z.slot2
            v.wait_ge(C["b_s"].h, 2 * oc + 1)
            v.tensor_copy(sl[1][0:32, :], sl[0][32:64, :])
            v.tensor_copy(sl[1][32:64, :], sl[0][0:32, :])
            v.tensor_copy(sl[1][64:96, :], sl[0][96:128, :])
            v.tensor_copy(sl[1][96:128, :], sl[0][64:96, :])
            v.drain()
            v.tensor_mul(sl[3], sl[1], Z.sinsgn_t)
            v.tensor_mul(sl[4], sl[0], Z.cos_t)
            v.drain()
            C["b_v"].inc(v.tensor_add(sl[1], sl[3], sl[4]))
            v.drain()
            v.tensor_scalar_min(sl[3], sl[1], 0.0)
            C["b_v"].inc(v.tensor_scalar_max(sl[4], sl[1], 0.0))
            v.wait_ge(C["b_s"].h, 2 * oc + 2)
            if oc >= 2:
                v.wait_ge(C["b_sp0" if oc % 2 == 0 else "b_sp1"].h, 16 * (oc // 2))
            C["b_v"].inc(v.scalar_tensor_tensor(
                Z.qkspl[oc % 2], sl[2], 1.0, sl[4], ALU.mult, ALU.add,
                accum_out=Z.qg_sb[:, oc:oc + 1]))

    @block.sync
    def _(sync):
        for oc in range(NCH):
            sync.wait_ge(C["b_v"].h, 3 * oc + 3)
            C["b_sp0" if oc % 2 == 0 else "b_sp1"].inc(
                sync.dma_start(out=Z.qk_sp[oc], in_=Z.qkspl[oc % 2]), 16)

    if MAXPH < 3:
        _tail(block, C)
        return

    @block.vector
    def _(v):
        v.wait_ge(C["b_v"].h, 3 * NCH)
        v.drain()
        v.tensor_scalar_mul(Z.qg_bf[:], Z.qg_sb[:], 1.0 / S)
        v.drain()
        C["c_qg"].inc(v.engine_nop())



    @block.vector
    def _(v):
        v.wait_ge(C["c_qg"].h, 2)
        v.tensor_copy(Z.qg_tr[:], psA[0:16, 0:128])
        v.drain()
        C["c_qg"].inc(v.engine_nop())
        v.memset(Z.qg_arr[:], 0.0)
        v.drain()
        C["c_qg"].inc(v.engine_nop())

    @block.sync
    def _(sync):
        sync.wait_ge(C["c_qg"].h, 4)
        C["c_dma"].inc(sync.dma_start(out=Z.qg_dram, in_=Z.qg_tr[:]), 16)
        sync.wait_ge(C["c_dma"].h, 16)
        qgd = Z.qg_dram.rearrange("a b -> (a b)")
        for n in range(NH):
            g = n // 4
            ch = (g * 64) // 128
            pb = (g * 64) % 128
            C["c_dma"].inc(sync.dma_start(
                out=Z.qg_arr[pb:pb + 64, ch * 32 + n:ch * 32 + n + 1],
                in_=qgd[n * 64:(n + 1) * 64].rearrange("(a b) -> a b", b=1)), 16)
    QGA = C["c_dma"].n

    @block.gpsimd
    def _(gp):
        gp.wait_ge(C["b_pe"].h, 8)
        gp.wait_ge(C["wc_kv"].h, 32 * 16)
        for k in range(NCH):
            C["kv_w"].inc(gp.dma_start(out=Z.WA[:, k * 512:(k + 1) * 512],
                                       in_=Z.wk_bf[k * 128:(k + 1) * 128, :]), 16)
        for k in range(NCH):
            C["kv_w"].inc(gp.dma_start(
                out=Z.WA[:, 8192 + k * 512:8192 + (k + 1) * 512],
                in_=Z.wv_bf[k * 128:(k + 1) * 128, :]), 16)
    KVW = 32 * 16

    @block.tensor
    def _(t):
        t.wait_ge(C["kv_w"].h, KVW)
        t.wait_ge(C["b_v"].h, 3 * NCH)
        t.wait_ge(C["b_s"].h, 2 * NCH)
        t.wait_ge(C["b_sp0"].h, 16 * 8)
        t.wait_ge(C["b_sp1"].h, 16 * 8)
        for kc in range(4):
            if kc == 2:
                t.wait_ge(C["c_qg"].h, 1)
                t.wait_ge(C["k_s"].h, 1)
                C["c_qg"].inc(t.matmul(psA[0:16, 0:128], Z.qg_bf[:], Z.ident[:],
                                       start=True, stop=True))
            if kc >= 2:
                t.wait_ge(C["k_s"].h, 2 * (kc - 2) + 1)
                if kc == 2:
                    t.wait_ge(C["c_qg"].h, 3)
            grp = psA if kc % 2 == 0 else psB
            for k in range(NCH):
                wt = Z.WA[:, k * 512 + kc * 128:k * 512 + kc * 128 + 128]
                for j in range(4):
                    mm = t.matmul(grp[:, j * 512:(j + 1) * 512], wt,
                                  Z.xn1[:, k * 2048 + j * 512:k * 2048 + (j + 1) * 512],
                                  start=(k == 0), stop=(k == NCH - 1))
            C["k_pe"].inc(mm)

    @block.scalar
    def _(s):
        for kc in range(4):
            sl = Z.slot if kc % 2 == 0 else Z.slot2
            grp = psA if kc % 2 == 0 else psB
            s.wait_ge(C["k_pe"].h, kc + 1)
            C["k_s"].inc(s.activation(sl[0], grp[:], AF.Copy))
            s.wait_ge(C["k_v"].h, 3 * kc + 2)
            C["k_s"].inc(s.activation(sl[2], sl[3], AF.Exp))

    @block.vector
    def _(v):
        for kc in range(4):
            sl = Z.slot if kc % 2 == 0 else Z.slot2
            v.wait_ge(C["k_s"].h, 2 * kc + 1)
            v.tensor_copy(sl[1][0:32, :], sl[0][32:64, :])
            v.tensor_copy(sl[1][32:64, :], sl[0][0:32, :])
            v.tensor_copy(sl[1][64:96, :], sl[0][96:128, :])
            v.tensor_copy(sl[1][96:128, :], sl[0][64:96, :])
            v.drain()
            v.tensor_mul(sl[3], sl[1], Z.sinsgn_t)
            v.tensor_mul(sl[4], sl[0], Z.cos_t)
            v.drain()
            C["k_v"].inc(v.tensor_add(sl[1], sl[3], sl[4]))
            v.drain()
            v.tensor_scalar_min(sl[3], sl[1], 0.0)
            C["k_v"].inc(v.tensor_scalar_max(sl[4], sl[1], 0.0))
            v.wait_ge(C["k_s"].h, 2 * kc + 2)
            C["k_v"].inc(v.scalar_tensor_tensor(
                Z.kkf[:, kc * 2048:(kc + 1) * 2048], sl[2], 1.0, sl[4],
                ALU.mult, ALU.add))

    @block.tensor
    def _(t):
        t.wait_ge(C["k_s"].h, 8)
        t.wait_ge(C["k_v"].h, 12)
        for sc in range(NCH):
            if sc >= 8:
                t.wait_ge(C["v_ev"].h, sc - 8 + 1)
            grp = psA if (sc % 8) < 4 else psB
            bs = (sc % 4) * 512
            for k in range(NCH):
                mm = t.matmul(grp[:, bs:bs + 512],
                              Z.xn1[:, k * 2048 + sc * 128:k * 2048 + sc * 128 + 128],
                              Z.WA[:, 8192 + k * 512:8192 + (k + 1) * 512],
                              start=(k == 0), stop=(k == NCH - 1))
            C["v_pe"].inc(mm)

    @block.vector
    def _(v):
        for sc in range(NCH):
            v.wait_ge(C["v_pe"].h, sc + 1)
            grp = psA if (sc % 8) < 4 else psB
            bs = (sc % 4) * 512
            C["v_ev"].inc(v.tensor_copy(Z.vt[:, sc * 512:(sc + 1) * 512],
                                        grp[:, bs:bs + 512]))

    @block.gpsimd
    def _(gp):
        gp.wait_ge(C["b_pe"].h, NCH)
        gp.wait_ge(C["wc_ph"].h, 16 * 16)
        for k in range(NCH):
            C["pw0"].inc(gp.dma_start(
                out=Z.WB[:, k * 1024:(k + 1) * 1024],
                in_=Z.wphi_bf[k * 128:(k + 1) * 128, 0:1024]), 16)
        gp.wait_ge(C["v_pe"].h, NCH)
        for k in range(NCH):
            C["pw1"].inc(gp.dma_start(
                out=Z.WA[:, k * 1024:(k + 1) * 1024],
                in_=Z.wphi_bf[k * 128:(k + 1) * 128, 1024:2048]), 16)
    PWH = 16 * 16

    @block.tensor
    def _(t):
        t.wait_ge(C["c_dma"].h, QGA)
        t.wait_ge(C["k_v"].h, 12)
        t.wait_ge(C["v_ev"].h, NCH)
        for j in range(4):
            for g2 in range(4):
                mm = t.matmul(psB[0:32, j * 512:(j + 1) * 512],
                              Z.qg_arr[:, g2 * 32:(g2 + 1) * 32],
                              Z.kkf[:, g2 * 2048 + j * 512:g2 * 2048 + (j + 1) * 512],
                              start=(g2 == 0), stop=(g2 == 3))
        C["c_pe"].inc(mm)
        for kc in range(4):
            if kc >= 1:
                t.wait_ge(C["c_v"].h, 2 + kc)
            for j in range(16):
                mm = t.matmul(psA[:, j * 128:(j + 1) * 128],
                              Z.kkf[:, kc * 2048 + j * 128:kc * 2048 + j * 128 + 128],
                              Z.ident[:], start=True, stop=True)
            C["c_pe"].inc(mm)
        t.wait_ge(C["c_v"].h, 2)
        t.wait_ge(C["c_s"].h, 1)
        for q4 in range(4):
            if q4 >= 1:
                t.wait_ge(C["c_v"].h, 6 + q4)
            for i in range(4):
                sc = q4 * 4 + i
                mm = t.matmul(psB[:, i * 512:i * 512 + 32],
                              Z.alpha_bf[0:32, sc * 128:(sc + 1) * 128],
                              Z.ident[0:32, 0:32], start=True, stop=True)
            C["c_pe"].inc(mm)

    @block.tensor
    def _(t):
        t.wait_ge(C["v_ev"].h, NCH)
        t.wait_ge(C["c_pe"].h, 9)
        t.wait_ge(C["c_v"].h, 10)
        for oc in range(NCH):
            t.wait_ge(C["pw0" if oc < 8 else "pw1"].h, PWH)
            if oc >= 2:
                t.wait_ge(C["p_s"].h, oc - 1)
            grp = psA if oc % 2 == 0 else psB
            for k in range(NCH):
                if oc < 8:
                    wt = Z.WB[:, k * 1024 + oc * 128:k * 1024 + oc * 128 + 128]
                else:
                    wt = Z.WA[:, k * 1024 + (oc - 8) * 128:k * 1024 + (oc - 8) * 128 + 128]
                for j in range(4):
                    mm = t.matmul(grp[:, j * 512:(j + 1) * 512], wt,
                                  Z.xn1[:, k * 2048 + j * 512:k * 2048 + (j + 1) * 512],
                                  start=(k == 0), stop=(k == NCH - 1))
            C["p_pe"].inc(mm)

    @block.scalar
    def _(s):
        s.wait_ge(C["c_v"].h, 1)
        C["c_s"].inc(s.activation(Z.exp_row[0:32, :], psB[0:32, :], AF.Exp,
                                  bias=Z.sm_cols[:, 0:1],
                                  accum_out=Z.sm_cols[:, 1:2]))

    @block.scalar
    def _(s):
        for oc in range(NCH):
            s.wait_ge(C["p_pe"].h, oc + 1)
            if oc >= 2:
                s.wait_ge(C["p_sp0" if oc % 2 == 0 else "p_sp1"].h, 16 * (oc // 2))
            grp = psA if oc % 2 == 0 else psB
            C["p_s"].inc(s.activation(Z.phspl[oc % 2], grp[:], AF.Copy))

    @block.sync
    def _(sync):
        for oc in range(NCH):
            sync.wait_ge(C["p_s"].h, oc + 1)
            C["p_sp0" if oc % 2 == 0 else "p_sp1"].inc(
                sync.dma_start(out=Z.phi_sp[oc], in_=Z.phspl[oc % 2]), 16)

    if MAXPH < 4:
        _tail(block, C)
        return

    @block.tensor
    def _(t):
        t.wait_ge(C["p_s"].h, NCH)
        for sc in range(NCH):
            t.wait_ge(C["c_v"].h, 12 + sc)
            for c2 in range(NCH):
                mm = t.matmul(psA[:, c2 * 128:c2 * 128 + 64],
                              Z.wks[sc % 6][:, c2 * 128:(c2 + 1) * 128],
                              Z.vt[:, sc * 512 + (c2 // 2) * 64:
                                   sc * 512 + (c2 // 2) * 64 + 64],
                              start=(sc == 0 and c2 % 4 == 0),
                              stop=(sc == NCH - 1 and c2 % 4 == 3))
            C["c_pe"].inc(mm)


    @block.vector
    def _(v):
        v.wait_ge(C["c_pe"].h, 1)
        v.reduce_max(Z.sm_cols[:, 0:1], psB[0:32, :], axis=AX.X, negate=True)
        v.drain()
        C["c_v"].inc(v.engine_nop())
        v.wait_ge(C["c_s"].h, 1)
        v.reciprocal(Z.sm_cols[:, 2:3], Z.sm_cols[:, 1:2])
        v.drain()
        v.tensor_scalar_mul(Z.sm_cols[:, 2:3], Z.sm_cols[:, 2:3], float(S))
        v.drain()
        v.tensor_scalar(Z.alpha_bf[0:32, :], Z.exp_row[0:32, :],
                        Z.sm_cols[:, 2:3], None, ALU.mult)
        v.drain()
        C["c_v"].inc(v.engine_nop())
        for kc in range(4):
            v.wait_ge(C["c_pe"].h, 2 + kc)
            outv = Z.kkt.rearrange("p (a b) -> p a b", b=512)[:, :, kc * 128:kc * 128 + 128]
            inv = psA[:].rearrange("p (a b) -> p a b", b=128)
            C["c_v"].inc(v.tensor_copy(outv, inv))
        for q4 in range(4):
            v.wait_ge(C["c_pe"].h, 6 + q4)
            outv = Z.alpha_tm.rearrange("p (a b) -> p a b", b=32)[:, q4 * 4:(q4 + 1) * 4, :]
            inv = psB[:].rearrange("p (a b) -> p a b", b=512)[:, :, 0:32]
            C["c_v"].inc(v.tensor_copy(outv, inv))
        v.wait_ge(C["c_v"].h, 10)
        v.drain()
        C["c_v"].inc(v.memset(Z.bd_sb[:], 0.0))
        for sc in range(NCH):
            if sc >= 6:
                v.wait_ge(C["c_pe"].h, 4 + sc)
            in0 = Z.kkt[:, sc * 512:(sc + 1) * 512].rearrange(
                "p (g j) -> p g j", g=8)
            in0 = in0.unsqueeze(2).broadcast_to([128, 8, 4, 64])
            in1 = Z.alpha_tm[:, sc * 32:(sc + 1) * 32].rearrange(
                "p (g n) -> p g n", g=8)
            in1 = in1.unsqueeze(3).broadcast_to([128, 8, 4, 64])
            outv = Z.wks[sc % 6][:].rearrange("p (g n j) -> p g n j", g=8, n=4)
            C["c_v"].inc(v.tensor_tensor(outv, in0, in1, ALU.mult))
        v.wait_ge(C["c_pe"].h, 9 + NCH)
        for c2 in range(NCH):
            v.tensor_copy(Z.bd_sb[0:64, c2 * 128:c2 * 128 + 64],
                          psA[0:64, c2 * 128:c2 * 128 + 64])
            C["c_v"].inc(v.tensor_copy(
                Z.bd_sb[64:128, c2 * 128 + 64:c2 * 128 + 128],
                psA[64:128, c2 * 128:c2 * 128 + 64]))
    CV_END = 43

    if DEBUG:
        @block.sync
        def _(sync):
            sync.wait_ge(C["c_v"].h, CV_END)
            C["dbg"].inc(sync.dma_start(out=Z.dbg["dbg_kkt"], in_=Z.kkt), 16)
            C["dbg"].inc(sync.dma_start(out=Z.dbg["dbg_kkf"], in_=Z.kkf), 16)
            C["dbg"].inc(sync.dma_start(out=Z.dbg["dbg_vt"], in_=Z.vt), 16)
            C["dbg"].inc(sync.dma_start(out=Z.dbg["dbg_alpha"], in_=Z.alpha_tm[:]), 16)
            C["dbg"].inc(sync.dma_start(out=Z.dbg["dbg_qgarr"], in_=Z.qg_arr[:]), 16)
            C["dbg"].inc(sync.dma_start(out=Z.dbg["dbg_bd"], in_=Z.bd_sb[:]), 16)
    if MAXPH < 5:
        _tail(block, C, [("c_v", CV_END)])
        return

    @block.sync
    def _(sync):
        sync.wait_ge(C["p_pe"].h, NCH)
        for c2 in range(NCH):
            if c2 >= 2:
                sync.wait_ge(C["d_pe"].h, c2 - 1)
            C["d_qk0" if c2 % 2 == 0 else "d_qk1"].inc(
                sync.dma_start(out=Z.qkroll[c2 % 2], in_=Z.qk_sp[c2]), 16)

    @block.scalar
    def _(s):
        s.wait_ge(C["p_pe"].h, NCH)
        s.wait_ge(C["p_sp0"].h, 16 * 8)
        s.wait_ge(C["p_sp1"].h, 16 * 8)
        for c2 in range(NCH):
            if c2 >= 2:
                s.wait_ge(C["d_v"].h, 2 * (c2 - 2) + 2)
            C["d_ph0" if c2 % 2 == 0 else "d_ph1"].inc(
                s.dma_start(out=Z.phiroll[c2 % 2], in_=Z.phi_sp[c2]), 16)

    @block.tensor
    def _(t):
        t.wait_ge(C["c_v"].h, CV_END)
        for c2 in range(NCH):
            t.wait_ge(C["d_qk0" if c2 % 2 == 0 else "d_qk1"].h, 16 * (c2 // 2 + 1))
            if c2 >= 2:
                t.wait_ge(C["d_v"].h, 2 * (c2 - 2) + 1)
            grp = psA if c2 % 2 == 0 else psB
            for j in range(4):
                mm = t.matmul(grp[:, j * 512:(j + 1) * 512],
                              Z.bd_sb[:, c2 * 128:(c2 + 1) * 128],
                              Z.qkroll[c2 % 2][:, j * 512:(j + 1) * 512],
                              start=True, stop=True)
            C["d_pe"].inc(mm)

    @block.vector
    def _(v):
        for c2 in range(NCH):
            grp = psA if c2 % 2 == 0 else psB
            v.wait_ge(C["d_pe"].h, c2 + 1)
            C["d_v"].inc(v.tensor_copy(Z.ra_ev[c2 % 2], grp[:]))
            v.drain()
            v.wait_ge(C["d_ph0" if c2 % 2 == 0 else "d_ph1"].h, 16 * (c2 // 2 + 1))
            C["d_v"].inc(v.tensor_mul(Z.ctx[:, c2 * 2048:(c2 + 1) * 2048],
                                      Z.phiroll[c2 % 2], Z.ra_ev[c2 % 2]))

    if DEBUG:
        @block.sync
        def _(sync):
            sync.wait_ge(C["d_v"].h, 2 * NCH)
            C["dbg"].inc(sync.dma_start(out=Z.dbg["dbg_ctx"], in_=Z.ctx), 16)
    if MAXPH < 6:
        _tail(block, C, [("d_v", 2 * NCH)])
        return

    @block.gpsimd
    def _(gp):
        gp.wait_ge(C["p_pe"].h, 8)
        gp.wait_ge(C["wc_wo"].h, 16 * 16)
        for k in range(NCH):
            C["wo_w"].inc(gp.dma_start(out=Z.WB[:, k * 1024:(k + 1) * 1024],
                                       in_=Z.wo_bf[k * 128:(k + 1) * 128, 0:1024]), 16)
        gp.wait_ge(C["p_pe"].h, NCH)
        gp.wait_ge(C["c_pe"].h, 9)
        for k in range(NCH):
            C["wo_w"].inc(gp.dma_start(out=Z.WA[:, k * 1024:(k + 1) * 1024],
                                       in_=Z.wo_bf[k * 128:(k + 1) * 128, 1024:2048]), 16)
    WO = 32 * 16

    @block.tensor
    def _(t):
        t.wait_ge(C["wo_w"].h, WO)
        t.wait_ge(C["d_v"].h, 2 * NCH)
        for sc in range(NCH):
            if sc >= 2:
                t.wait_ge(C["e_s"].h, sc - 1)
            grp = psA if sc % 2 == 0 else psB
            for k in range(NCH):
                lh = Z.ctx[:, k * 2048 + sc * 128:k * 2048 + sc * 128 + 128]
                for nt in range(4):
                    if nt < 2:
                        rhs = Z.WB[:, k * 1024 + nt * 512:k * 1024 + (nt + 1) * 512]
                    else:
                        rhs = Z.WA[:, k * 1024 + (nt - 2) * 512:k * 1024 + (nt - 1) * 512]
                    mm = t.matmul(grp[:, nt * 512:(nt + 1) * 512], lh, rhs,
                                  start=(k == 0), stop=(k == NCH - 1))
            C["e_pe"].inc(mm)

    @block.scalar
    def _(s):
        for sc in range(NCH):
            s.wait_ge(C["e_pe"].h, sc + 1)
            if sc >= 2:
                s.wait_ge(C["e_sp0" if sc % 2 == 0 else "e_sp1"].h, 16 * (sc // 2))
            grp = psA if sc % 2 == 0 else psB
            C["e_s"].inc(s.activation(Z.ao_ev[sc % 2], grp[:], AF.Copy))

    @block.sync
    def _(sync):
        for sc in range(NCH):
            sync.wait_ge(C["e_s"].h, sc + 1)
            C["e_sp0" if sc % 2 == 0 else "e_sp1"].inc(
                sync.dma_start(out=Z.ao_sp[sc], in_=Z.ao_ev[sc % 2]), 16)

    if MAXPH < 7:
        _tail(block, C)
        return

    @block.sync
    def _(sync):
        sync.wait_ge(C["e_sp0"].h, 16 * 8)
        sync.wait_ge(C["e_sp1"].h, 16 * 8)
        C["f0_r"].inc(sync.dma_start(out=Z.rows_f32, in_=Z.rows3_h), 16)
        sync.wait_ge(C["wc_wf"].h, 16 * 16)
        for oc in range(NCH):
            if oc >= 2:
                sync.wait_ge(C["f0_s"].h, oc - 1)
                sync.wait_ge(C["f0_v"].h, 5 + 3 * (oc - 2))
            C["f0_w0" if oc % 2 == 0 else "f0_w1"].inc(
                sync.dma_start(out=Z.wf_roll[oc % 2][:],
                               in_=Z.wf_bf[oc * 128:(oc + 1) * 128, :]), 16)
        sync.wait_ge(C["f0_v"].h, 56)
        C["f0_r2"].inc(sync.dma_start(
            out=Z.row_sp.rearrange("o (a b) -> (o a) b", a=16),
            in_=Z.rowl_ev[:]), 16)
        sync.wait_ge(C["f0_r2"].h, 16)
        C["f0_r2"].inc(sync.dma_start(out=Z.bt_aux[:], in_=Z.row_sp), 16)

    @block.scalar
    def _(s):
        for oc in range(NCH):
            s.wait_ge(C["f0_w0" if oc % 2 == 0 else "f0_w1"].h, 16 * (oc // 2 + 1))
            if oc >= 2:
                s.wait_ge(C["f0_v"].h, 7 + 3 * (oc - 2))
            C["f0_s"].inc(s.activation(Z.wf_sgn[oc % 2], Z.wf_roll[oc % 2][:],
                                       AF.Sign))
            if oc >= 1:
                j = oc - 1
                s.wait_ge(C["f0_pe"].h, j + 3)
                grp = psA if j % 2 == 0 else psB
                outv = Z.qwt.rearrange("p (a b) -> p a b",
                                       b=2048)[:, :, j * 128:(j + 1) * 128]
                inv = grp[:].rearrange("p (a b) -> p a b", b=128)
                C["f0_e"].inc(s.activation(outv, inv, AF.Copy))
        s.wait_ge(C["f0_pe"].h, NCH + 2)
        j = NCH - 1
        grp = psA if j % 2 == 0 else psB
        outv = Z.qwt.rearrange("p (a b) -> p a b",
                               b=2048)[:, :, j * 128:(j + 1) * 128]
        inv = grp[:].rearrange("p (a b) -> p a b", b=128)
        C["f0_e"].inc(s.activation(outv, inv, AF.Copy))

    @block.vector
    def _(v):
        v.wait_ge(C["e_s"].h, NCH)
        v.wait_ge(C["f0_r"].h, 16)
        v.tensor_copy(Z.rows_bf, Z.rows_f32)
        v.drain()
        C["f0_v"].inc(v.engine_nop())
        C["f0_v"].inc(v.memset(Z.ones_bf[:], 1.0))
        v.wait_ge(C["f0_pe"].h, 2)
        C["f0_v"].inc(v.tensor_copy(Z.lnw_bc, psA[:]))
        C["f0_v"].inc(v.tensor_copy(Z.lnb_bc, psB[:]))
        v.drain()
        for oc in range(NCH):
            v.wait_ge(C["f0_w0" if oc % 2 == 0 else "f0_w1"].h, 16 * (oc // 2 + 1))
            C["f0_v"].inc(v.tensor_reduce(
                Z.maxw_sb[:, oc:oc + 1], Z.wf_roll[oc % 2][:], axis=AX.X,
                op=ALU.max, apply_absolute_value=True))
            v.wait_ge(C["f0_s"].h, oc + 1)
            if oc >= 2:
                v.wait_ge(C["f0_pe"].h, oc + 1)
            C["f0_v"].inc(v.scalar_tensor_tensor(
                Z.wf_sg2[oc % 2], Z.wf_sgn[oc % 2], Z.gamma_sb[:, oc:oc + 1],
                Z.lnw_bc, ALU.mult, ALU.mult))
            C["f0_v"].inc(v.scalar_tensor_tensor(
                Z.f0_junk[oc % 2], Z.wf_sgn[oc % 2], Z.gamma_sb[:, oc:oc + 1],
                Z.lnb_bc, ALU.mult, ALU.mult,
                accum_out=Z.coll_sb[:, oc:oc + 1]))
        v.drain()
        v.tensor_reduce(Z.maxw_sb[:, 16:17], Z.maxw_sb[:, 0:16],
                        axis=AX.X, op=ALU.max)
        v.drain()
        v.tensor_copy(Z.maxw_bf[:], Z.maxw_sb[:, 16:17])
        v.drain()
        C["f0_v"].inc(v.engine_nop())
        v.tensor_copy(Z.coll_bf[:], Z.coll_sb[:])
        v.drain()
        C["f0_v"].inc(v.engine_nop())
        v.wait_ge(C["f0_pe"].h, NCH + 3)
        v.tensor_reduce(Z.scale_col[0:1, 0:1], psB[0:1, 0:128], axis=AX.X,
                        op=ALU.max)
        v.drain()
        v.reciprocal(Z.maxw_sb[0:1, 17:18], Z.scale_col[0:1, 0:1])
        v.drain()
        C["f0_v"].inc(v.engine_nop())
        v.wait_ge(C["f0_pe"].h, NCH + 4)
        C["f0_v"].inc(v.tensor_copy(Z.rowl_ev[:], psA[0:16, 0:128]))
        v.wait_ge(C["f0_r2"].h, 32)
        v.tensor_scalar(Z.bt_tmp[:], Z.rows_bf[:, 4096:6144],
                        Z.maxw_sb[0:1, 17:18], None, ALU.mult)
        v.drain()
        C["f0_v"].inc(v.tensor_add(Z.bt_row[:], Z.bt_tmp[:],
                                   Z.bt_aux[:]))
    FV_MX = 53
    FV_BETA = 57

    @block.tensor
    def _(t):
        t.wait_ge(C["e_s"].h, NCH)
        t.wait_ge(C["f0_v"].h, 2)
        for nt in range(4):
            mm = t.matmul(psA[:, nt * 512:(nt + 1) * 512], Z.ones_bf[:],
                          Z.rows_bf[:, nt * 512:(nt + 1) * 512],
                          start=True, stop=True)
        C["f0_pe"].inc(mm)
        for nt in range(4):
            mm = t.matmul(psB[:, nt * 512:(nt + 1) * 512], Z.ones_bf[:],
                          Z.rows_bf[:, 2048 + nt * 512:2048 + (nt + 1) * 512],
                          start=True, stop=True)
        C["f0_pe"].inc(mm)

    @block.tensor
    def _(t):
        for oc in range(NCH):
            t.wait_ge(C["f0_v"].h, 6 + 3 * oc)
            if oc >= 2:
                t.wait_ge(C["f0_e"].h, oc - 1)
            grp = psA if oc % 2 == 0 else psB
            for kc in range(16):
                mm = t.matmul(grp[:, kc * 128:(kc + 1) * 128],
                              Z.wf_sg2[oc % 2][:, kc * 128:(kc + 1) * 128],
                              Z.ident[:], start=True, stop=True)
            C["f0_pe"].inc(mm)

    @block.tensor
    def _(t):
        t.wait_ge(C["f0_v"].h, FV_MX)
        t.wait_ge(C["f0_e"].h, 16)
        C["f0_pe"].inc(t.matmul(psB[0:1, 0:128], Z.maxw_bf[:], Z.ident[:],
                                start=True, stop=True))
        t.wait_ge(C["f0_v"].h, 54)
        C["f0_pe"].inc(t.matmul(psA[0:16, 0:128], Z.coll_bf[:],
                                Z.ident[:], start=True, stop=True))
    PE_BETA = NCH + 4

    if DEBUG:
        @block.sync
        def _(sync):
            sync.wait_ge(C["f0_v"].h, FV_BETA)
            C["dbg"].inc(sync.dma_start(out=Z.dbg["dbg_qwt"], in_=Z.qwt), 16)
            C["dbg"].inc(sync.dma_start(out=Z.dbg["dbg_scale"],
                                        in_=Z.scale_col[:, 0:1]), 16)
    if MAXPH < 8:
        _tail(block, C, [("f0_v", FV_BETA)])
        return

    fvV = {k: {} for k in range(1, 10)}
    fvG = {k: {} for k in (11, 12, 13)}
    pos = 0
    for k in range(1, 9):
        pos += 1; fvV[k][0] = pos
    for sc in range(NCH):
        pos += 1; fvV[9][sc] = pos
        if sc + 1 < NCH:
            for k in range(1, 9):
                pos += 1; fvV[k][sc + 1] = pos
        if sc >= 1:
            for k in (11, 12, 13):
                pos += 1; fvG[k][sc - 1] = pos
    for k in (11, 12, 13):
        pos += 1; fvG[k][NCH - 1] = pos
    fsS = {k: {} for k in (2, 3, 5, 6, 7, 8, 10)}
    pos = 0
    for k in (2, 3, 5):
        pos += 1; fsS[k][0] = pos
    for sc in range(NCH):
        pos += 1; fsS[10][sc] = pos
        if sc + 1 < NCH:
            for k in (2, 3, 5):
                pos += 1; fsS[k][sc + 1] = pos
        if sc >= 1:
            for k in (6, 7, 8):
                pos += 1; fsS[k][sc - 1] = pos
    for k in (6, 7, 8):
        pos += 1; fsS[k][NCH - 1] = pos
    fp_P2 = {}
    pos = 0
    for sc in range(NCH):
        pos += 1; fp_P2[sc] = pos

    def pe_T(sc):
        return 2 * sc + 1

    def pe_F3(sc):
        return 2 * sc + 2

    f_x_at, f_ao_at = [], []

    @block.sync
    def _(sync):
        sync.wait_ge(C["f0_pe"].h, PE_BETA)
        sync.wait_ge(C["f0_v"].h, FV_BETA)
        sync.wait_ge(C["f0_s"].h, NCH)
        for sc in range(NCH):
            if sc >= 2:
                sync.wait_ge(C["f_v"].h, fvV[3][sc - 2])
                sync.wait_ge(C["f_s"].h, fsS[2][sc - 2])
            f_ao_at.append(C["f_ao0" if sc % 2 == 0 else "f_ao1"].inc(
                sync.dma_start(out=Z.f_ao[sc % 2][:], in_=Z.ao_sp[sc]), 16))
            if sc >= 2:
                sync.wait_ge(C["f_v"].h, fvV[4][sc - 2])
            f_x_at.append(C["f_x0" if sc % 2 == 0 else "f_x1"].inc(sync.dma_start(
                out=Z.f_x[sc % 2][:], in_=Z.x_h[sc * 128:(sc + 1) * 128, :]), 16))
        sync.wait_ge(C["f_out0"].h, 16 * 8)
        sync.wait_ge(C["f_out1"].h, 16 * 8)

    @block.scalar
    def _(s):
        st = Z.stat
        s.wait_ge(C["e_sp0"].h, 16 * 8)
        s.wait_ge(C["e_sp1"].h, 16 * 8)
        s.wait_ge(C["f0_v"].h, FV_MX)

        def s_f1(s, sc):
            b = 32 * (sc % 2)
            s.wait_ge(C["f_ao0" if sc % 2 == 0 else "f_ao1"].h, f_ao_at[sc])
            C["f_s"].inc(s.activation(Z.f_junk, Z.f_ao[sc % 2][:], AF.Square,
                                      accum_out=st[:, b + 1:b + 2]))
            s.wait_ge(C["f_v"].h, fvV[2][sc])
            s.activation(st[:, b + 11:b + 12], st[:, b + 4:b + 5], AF.Ln,
                         bias=float(EPS))
            s.drain()
            C["f_s"].inc(s.activation(Z.r2_sb[:, sc:sc + 1], st[:, b + 11:b + 12],
                                      AF.Exp, scale=-0.5))
            s.wait_ge(C["f_v"].h, fvV[7][sc])
            s.activation(st[:, b + 11:b + 12], st[:, b + 10:b + 11], AF.Ln,
                         scale=float(1.0 + EPS), bias=float(EPS * EPS))
            s.drain()
            C["f_s"].inc(s.activation(st[:, b + 12:b + 13], st[:, b + 11:b + 12],
                                      AF.Exp, scale=-0.5))

        def s_evac(s, sc):
            s.wait_ge(C["f_pe"].h, pe_T(sc))
            grp = psA if sc % 2 == 0 else psB
            C["f_s"].inc(s.activation(Z.xn2w[sc % 2][:, 1024:2048],
                                      grp[:, 1024:2048], AF.Copy))

        def s_gtail(s, j):
            b = 32 * (j % 2)
            grp = psA if j % 2 == 0 else psB
            s.wait_ge(C["f_pe"].h, pe_F3(j))
            C["f_s"].inc(s.activation(Z.f_junk, grp[:], AF.Copy,
                                      accum_out=st[:, b + 13:b + 14]))
            s.drain()
            C["f_s"].inc(s.activation(Z.f_junk, grp[:], AF.Square,
                                      accum_out=st[:, b + 14:b + 15]))
            s.wait_ge(C["f_v"].h, fvG[12][j])
            s.activation(st[:, b + 18:b + 19], st[:, b + 17:b + 18], AF.Ln,
                         bias=float(EPS))
            s.drain()
            C["f_s"].inc(s.activation(st[:, b + 19:b + 20], st[:, b + 18:b + 19],
                                      AF.Exp, scale=-0.5))

        s_f1(s, 0)
        for sc in range(NCH):
            s_evac(s, sc)
            if sc + 1 < NCH:
                s_f1(s, sc + 1)
            if sc >= 1:
                s_gtail(s, sc - 1)
        s_gtail(s, NCH - 1)

    @block.gpsimd
    def _(gp):
        for j in range(NCH):
            gp.wait_ge(C["f_v"].h, fvG[13][j])
            if j >= 2:
                gp.wait_ge(C["f_out0" if j % 2 == 0 else "f_out1"].h,
                           16 * (j // 2))
            gp.drain()
            C["f_p"].inc(gp.tensor_tensor(Z.g_x2[j % 2][:], Z.f_x1[j % 3][:],
                                          Z.g_t[j % 2][:], ALU.add))
            gp.drain()
            C["f_out0" if j % 2 == 0 else "f_out1"].inc(gp.dma_start(
                out=Z.y_h[j * 128:(j + 1) * 128, :], in_=Z.g_x2[j % 2][:]), 16)

    @block.vector
    def _(v):
        st = Z.stat
        v.wait_ge(C["f0_pe"].h, PE_BETA)

        def v_f1(v, sc):
            b = 32 * (sc % 2)
            v.wait_ge(C["f_ao0" if sc % 2 == 0 else "f_ao1"].h, f_ao_at[sc])
            v.reduce_sum(st[:, b + 0:b + 1], Z.f_ao[sc % 2][:], axis=AX.X)
            v.drain()
            C["f_v"].inc(v.tensor_scalar_mul(Z.m2_sb[:, sc:sc + 1],
                                             st[:, b + 0:b + 1], 1.0 / D))
            v.wait_ge(C["f_s"].h, fsS[2][sc])
            v.tensor_scalar_mul(st[:, b + 2:b + 3], st[:, b + 1:b + 2], 1.0 / D)
            v.drain()
            v.tensor_mul(st[:, b + 3:b + 4], Z.m2_sb[:, sc:sc + 1],
                         Z.m2_sb[:, sc:sc + 1])
            v.drain()
            C["f_v"].inc(v.tensor_sub(st[:, b + 4:b + 5], st[:, b + 2:b + 3],
                                      st[:, b + 3:b + 4]))
            v.wait_ge(C["f_s"].h, fsS[3][sc])
            C["f_v"].inc(v.tensor_scalar(Z.f_t[0][:], Z.f_ao[sc % 2][:],
                                         Z.m2_sb[:, sc:sc + 1],
                                         Z.r2_sb[:, sc:sc + 1],
                                         ALU.subtract, ALU.mult))
            v.wait_ge(C["f_x0" if sc % 2 == 0 else "f_x1"].h, f_x_at[sc])
            if sc >= 3:
                v.wait_ge(C["f_p"].h, fp_P2[sc - 3])
            v.drain()
            C["f_v"].inc(v.scalar_tensor_tensor(
                Z.f_x1[sc % 3][:], Z.f_x[sc % 2][:], 1.0, Z.f_t[0][:],
                ALU.mult, ALU.add, accum_out=st[:, b + 5:b + 6]))
            v.drain()
            C["f_v"].inc(v.scalar_tensor_tensor(
                Z.f_u, Z.f_x1[sc % 3][:], 1.0, Z.f_x1[sc % 3][:],
                ALU.mult, ALU.mult, accum_out=st[:, b + 7:b + 8]))
            v.drain()
            C["f_v"].inc(v.tensor_scalar_mul(st[:, b + 6:b + 7],
                                             st[:, b + 5:b + 6], 1.0 / D))
            v.tensor_scalar_mul(st[:, b + 8:b + 9], st[:, b + 7:b + 8], 1.0 / D)
            v.drain()
            v.tensor_mul(st[:, b + 9:b + 10], st[:, b + 6:b + 7], st[:, b + 6:b + 7])
            v.drain()
            C["f_v"].inc(v.tensor_sub(st[:, b + 10:b + 11], st[:, b + 8:b + 9],
                                      st[:, b + 9:b + 10]))
            v.wait_ge(C["f_s"].h, fsS[5][sc])
            if sc >= 2:
                v.wait_ge(C["f_pe"].h, pe_T(sc - 2))
            C["f_v"].inc(v.tensor_scalar(Z.f_stg[sc % 2], Z.f_x1[sc % 3][:],
                                         st[:, b + 6:b + 7], st[:, b + 12:b + 13],
                                         ALU.subtract, ALU.mult))

        def v_gtail(v, j):
            b = 32 * (j % 2)
            grp = psA if j % 2 == 0 else psB
            v.wait_ge(C["f_s"].h, fsS[6][j])
            v.drain()
            C["f_v"].inc(v.tensor_scalar_mul(st[:, b + 15:b + 16],
                                             st[:, b + 13:b + 14],
                                             1.0 / D))
            v.wait_ge(C["f_s"].h, fsS[7][j])
            v.tensor_scalar_mul(st[:, b + 16:b + 17], st[:, b + 14:b + 15], 1.0 / D)
            v.drain()
            v.tensor_mul(st[:, b + 21:b + 22], st[:, b + 15:b + 16],
                         st[:, b + 15:b + 16])
            v.drain()
            C["f_v"].inc(v.tensor_sub(st[:, b + 17:b + 18], st[:, b + 16:b + 17],
                                      st[:, b + 21:b + 22]))
            v.wait_ge(C["f_s"].h, fsS[8][j])
            if j >= 2:
                v.wait_ge(C["f_p"].h, fp_P2[j - 2])
            C["f_v"].inc(v.tensor_scalar(Z.g_t[j % 2][:], grp[:],
                                         st[:, b + 15:b + 16],
                                         st[:, b + 19:b + 20],
                                         ALU.subtract, ALU.mult))

        def v_evac(v, sc):
            v.wait_ge(C["f_pe"].h, pe_T(sc))
            grp = psA if sc % 2 == 0 else psB
            C["f_v"].inc(v.tensor_copy(Z.xn2w[sc % 2][:, 0:1024],
                                       grp[:, 0:1024]))

        v_f1(v, 0)
        for sc in range(NCH):
            v_evac(v, sc)
            if sc + 1 < NCH:
                v_f1(v, sc + 1)
            if sc >= 1:
                v_gtail(v, sc - 1)
        v_gtail(v, NCH - 1)

    @block.tensor
    def _(t):
        for sc in range(NCH):
            t.wait_ge(C["f_v"].h, fvV[8][sc])
            if sc >= 2:
                t.wait_ge(C["f_v"].h, fvG[13][sc - 2])
            grp = psA if sc % 2 == 0 else psB
            for kc in range(16):
                mm = t.matmul(grp[:, kc * 128:(kc + 1) * 128],
                              Z.f_stg[sc % 2][:, kc * 128:(kc + 1) * 128],
                              Z.ident[:], start=True, stop=True)
            C["f_pe"].inc(mm)
            t.wait_ge(C["f_v"].h, fvV[9][sc])
            t.wait_ge(C["f_s"].h, fsS[10][sc])
            _emit_f3(Z, t, sc)

    _tail(block, C)


def _emit_f3(Z, t, sc):
    C = Z.C
    psA, psB = Z.psA, Z.psB
    grp = psA if sc % 2 == 0 else psB
    for k in range(NCH):
        lh = Z.xn2w[sc % 2][:, k * 128:(k + 1) * 128]
        for nt in range(4):
            mm = t.matmul(grp[:, nt * 512:(nt + 1) * 512], lh,
                          Z.qwt[:, k * 2048 + nt * 512:k * 2048 + (nt + 1) * 512],
                          start=(k == 0), stop=False)
    for nt in range(4):
        mm = t.matmul(grp[:, nt * 512:(nt + 1) * 512], Z.ones_bf[:],
                      Z.bt_row[:, nt * 512:(nt + 1) * 512],
                      start=False, stop=True)
    C["f_pe"].inc(mm)


def _tail(block, C, waits=None):
    @block.sync
    def _(sync):
        for name, ctr in C.items():
            if ctr.n > 0:
                sync.wait_ge(ctr.h, ctr.n)


_TABLES = None


def kernel(**inputs):
    global _TABLES
    if _TABLES is None:
        _TABLES = host_tables()
    cos_t, sinsgn_t, ident = _TABLES

    x = np.ascontiguousarray(inputs["L0_attn_hidden_states"])
    gamma = np.ascontiguousarray(
        inputs["L0_ffn_gamma"].reshape(16, 128).T).astype(np.float32)
    rows3 = np.concatenate([inputs["L0_ffn_ln_weight"],
                            inputs["L0_ffn_ln_bias"],
                            inputs["L0_ffn_beta"]]).reshape(1, 3 * D)
    rows3 = np.ascontiguousarray(rows3).astype(np.float32)

    base = {
        "wq": np.ascontiguousarray(inputs["L0_attn_Wq"]),
        "wk": np.ascontiguousarray(inputs["L0_attn_Wk"]),
        "wv": np.ascontiguousarray(inputs["L0_attn_Wv"]),
        "wphi": np.ascontiguousarray(inputs["L0_attn_Wphi"]),
        "wo": np.ascontiguousarray(inputs["L0_attn_Wo"]),
        "wffn": np.ascontiguousarray(inputs["L0_ffn_weight"]),
        "gamma": gamma,
        "rows3": rows3,
        "cos_t": np.ascontiguousarray(cos_t),
        "sinsgn_t": np.ascontiguousarray(sinsgn_t),
        "ident": np.ascontiguousarray(ident),
    }
    in_maps = [dict(base, x=np.ascontiguousarray(x[b])) for b in range(8)]

    nc = build_nc()
    res = run_bass_kernel_spmd(nc, in_maps, core_ids=list(range(8)))
    out = np.stack([res.results[b]["y"] for b in range(8)], axis=0)
    return out.astype(np.float32)


# version 13
# speedup vs baseline: 1.0164x; 1.0164x over previous
import contextlib
import os

import numpy as np
import ml_dtypes

import concourse.bass as bass
import concourse.mybir as mybir
from concourse.bass_utils import run_bass_kernel_spmd

F32 = mybir.dt.float32
BF16 = mybir.dt.bfloat16
U8 = mybir.dt.uint8
AF = mybir.ActivationFunctionType
ALU = mybir.AluOpType
AX = mybir.AxisListType

S = 2048
D = 2048
NCH = 16
NH = 32
HD = 64
DKV = 512
EPS = 1e-5

DEBUG = bool(int(os.environ.get("MK_DEBUG", "0")))
MAXPH = int(os.environ.get("MK_MAXPH", "99"))
F0CUT = int(os.environ.get("MK_F0CUT", "9"))
FCUT = int(os.environ.get("MK_FCUT", "9"))


def host_tables():
    inv = 10000.0 ** (-np.arange(0, HD, 2, dtype=np.float64) / HD)
    pos = np.arange(S, dtype=np.float64)
    ang = pos[None, :] * inv[:, None]
    p = np.arange(128)
    cos_t = np.cos(ang)[p % 32, :]
    sgn = np.where((p % 64) < 32, -1.0, 1.0)[:, None]
    sinsgn_t = sgn * np.sin(ang)[p % 32, :]
    return (cos_t.astype(np.float32).astype(ml_dtypes.bfloat16),
            sinsgn_t.astype(np.float32).astype(ml_dtypes.bfloat16),
            np.eye(128, dtype=np.float32).astype(ml_dtypes.bfloat16))


class Ctr:
    def __init__(self, handle):
        self.h = handle
        self.n = 0

    def inc(self, instr, k=1):
        instr.then_inc(self.h, k)
        self.n += k
        return self.n


class NS:
    pass


def build_nc():
    nc = bass.Bass("TRN2", target_bir_lowering=False)
    Z = NS()
    Z.nc = nc

    Z.x_h = nc.dram_tensor("x", [S, D], F32, kind="ExternalInput").ap()
    Z.wq_h = nc.dram_tensor("wq", [D, D], F32, kind="ExternalInput").ap()
    Z.wk_h = nc.dram_tensor("wk", [D, DKV], F32, kind="ExternalInput").ap()
    Z.wv_h = nc.dram_tensor("wv", [D, DKV], F32, kind="ExternalInput").ap()
    Z.wphi_h = nc.dram_tensor("wphi", [D, D], F32, kind="ExternalInput").ap()
    Z.wo_h = nc.dram_tensor("wo", [D, D], F32, kind="ExternalInput").ap()
    Z.wffn_h = nc.dram_tensor("wffn", [D, D], F32, kind="ExternalInput").ap()
    Z.gamma_h = nc.dram_tensor("gamma", [128, 16], F32, kind="ExternalInput").ap()
    Z.rows3_h = nc.dram_tensor("rows3", [1, 3 * D], F32, kind="ExternalInput").ap()
    Z.cos_h = nc.dram_tensor("cos_t", [128, S], BF16, kind="ExternalInput").ap()
    Z.sin_h = nc.dram_tensor("sinsgn_t", [128, S], BF16, kind="ExternalInput").ap()
    Z.ident_h = nc.dram_tensor("ident", [128, 128], BF16, kind="ExternalInput").ap()
    Z.y_h = nc.dram_tensor("y", [S, D], F32, kind="ExternalOutput").ap()
    spill_kind = "ExternalOutput" if DEBUG else "Internal"
    Z.row_sp = nc.dram_tensor("row_sp", [1, 2048], BF16, kind="Internal").ap()
    Z.wk_bf = nc.dram_tensor("wk_bf", [D, DKV], BF16, kind="Internal").ap()
    Z.wv_bf = nc.dram_tensor("wv_bf", [D, DKV], BF16, kind="Internal").ap()
    Z.wphi_bf = nc.dram_tensor("wphi_bf", [D, D], BF16, kind="Internal").ap()
    Z.wo_bf = nc.dram_tensor("wo_bf", [D, D], BF16, kind="Internal").ap()
    Z.wf_bf = nc.dram_tensor("wf_bf", [D, D], BF16, kind="Internal").ap()
    Z.qk_sp = nc.dram_tensor("qk_sp", [NCH, 128, S], BF16, kind=spill_kind).ap()
    Z.phi_sp = nc.dram_tensor("phi_sp", [NCH, 128, S], BF16, kind=spill_kind).ap()
    Z.ao_sp = nc.dram_tensor("ao_sp", [NCH, 128, D], BF16, kind=spill_kind).ap()
    Z.qg_dram = nc.dram_tensor("qg_dram", [16, 128], BF16, kind=spill_kind).ap()
    Z.dbg = {}
    if DEBUG:
        for nm, sh, dt in [("dbg_xn1", [128, NCH * S], BF16),
                           ("dbg_kkt", [128, NCH * DKV], BF16),
                           ("dbg_kkf", [128, 4 * S], BF16),
                           ("dbg_vt", [128, NCH * DKV], BF16),
                           ("dbg_alpha", [128, 512], BF16),
                           ("dbg_bd", [128, 2048], BF16),
                           ("dbg_ctx", [128, NCH * S], BF16),
                           ("dbg_qwt", [128, NCH * D], BF16),
                           ("dbg_qgarr", [128, 128], BF16),
                           ("dbg_scale", [128, 1], F32)]:
            Z.dbg[nm] = nc.dram_tensor(nm, sh, dt, kind="ExternalOutput").ap()

    es = contextlib.ExitStack()

    def sbt(name, shape, dtype):
        return es.enter_context(nc.sbuf_tensor(name, shape, dtype))

    R0 = sbt("R0", [128, 65536], U8)
    R1 = sbt("R1", [128, 65536], U8)
    R2 = sbt("R2", [128, 32768], U8)
    R3 = sbt("R3", [128, 32768], U8)
    SM1 = sbt("SM1", [128, 4096], U8)
    SM2 = sbt("SM2", [128, 4096], U8)
    Z.ident = sbt("ident_sb", [128, 128], BF16)
    Z.gamma_sb = sbt("gamma_sb", [128, 16], F32)
    Z.qg_sb = sbt("qg_sb", [128, 16], F32)
    Z.qg_bf = sbt("qg_bf", [128, 16], BF16)
    Z.qg_tr = sbt("qg_tr", [16, 128], BF16)
    Z.qg_arr = sbt("qg_arr", [128, 128], BF16)
    Z.m2_sb = sbt("m2_sb", [128, 16], F32)
    Z.r2_sb = sbt("r2_sb", [128, 16], F32)
    Z.bd_sb = sbt("bd_sb", [128, 16 * 128], BF16)
    Z.alpha_tm = sbt("alpha_tm", [128, 512], BF16)
    Z.scale_col = sbt("scale_col", [128, 1], F32)
    Z.scale_bf = sbt("scale_bf", [1, 1], BF16)
    Z.stat = sbt("stat", [128, 64], F32)
    Z.maxw_sb = sbt("maxw_sb", [128, 18], F32)
    Z.maxw_bf = sbt("maxw_bf", [128, 1], BF16)
    Z.sm_cols = sbt("sm_cols", [32, 4], F32)
    Z.coll_sb = sbt("coll_sb", [128, 16], F32)
    Z.coll_bf = sbt("coll_bf", [128, 16], BF16)
    Z.rowl_ev = sbt("rowl_ev", [16, 128], BF16)
    Z.consts = sbt("consts", [128, 8], F32)
    nc.const_aps.register(F32, float(EPS), Z.consts[:, 0:1])
    nc.const_aps.register(F32, -0.5, Z.consts[:, 1:2])
    nc.const_aps.register(F32, float(1.0 + EPS), Z.consts[:, 2:3])
    nc.const_aps.register(F32, float(EPS * EPS), Z.consts[:, 3:4])
    nc.const_aps.register(F32, 0.0, Z.consts[:, 4:5])
    Z.ones_bf = sbt("ones_bf", [1, 128], BF16)

    def carve(region, off, n, dtype):
        esz = 2 if dtype == BF16 else 4
        assert off + n * esz <= region.shape[1], (off, n, dtype)
        return region[:, off:off + n * esz].bitcast(dtype)

    Z.xn1 = carve(R0, 0, 32768, BF16)
    Z.qkroll = [carve(R0, i * 4096, 2048, BF16) for i in range(2)]
    Z.phiroll = [carve(R0, 8192 + i * 4096, 2048, BF16) for i in range(2)]
    Z.ra_ev = [carve(R0, 16384 + i * 4096, 2048, BF16) for i in range(2)]
    Z.ao_ev = [carve(R0, 24576 + i * 4096, 2048, BF16) for i in range(2)]
    Z.qwt = carve(R0, 0, 32768, BF16)
    Z.slot = [carve(R1, i * 4096, 2048, BF16) for i in range(5)]
    Z.slot2 = [carve(R1, 20480 + i * 4096, 2048, BF16) for i in range(5)]
    Z.qkspl = [carve(R1, 40960 + i * 4096, 2048, BF16) for i in range(2)]
    Z.xroll = [carve(R1, 49152 + i * 8192, 2048, F32) for i in range(2)]
    Z.vt = carve(R1, 0, 8192, BF16)
    Z.kkt = carve(R1, 16384, 8192, BF16)
    Z.phspl = [carve(R1, 32768 + i * 4096, 2048, BF16) for i in range(2)]
    Z.kkf = carve(R1, 40960, 8192, BF16)
    Z.wks = [carve(R1, 40960 + i * 4096, 2048, BF16) for i in range(6)]
    Z.ctx = carve(R1, 0, 32768, BF16)
    Z.xn2w = [carve(R1, i * 4096, 2048, BF16) for i in range(2)]
    Z.g_x2 = [carve(R1, 8192 + i * 8192, 2048, F32) for i in range(2)]
    Z.g_t = [carve(R1, 24576 + i * 4096, 2048, BF16) for i in range(2)]
    Z.f_x1 = [carve(R1, 32768 + i * 8192, 2048, F32) for i in range(3)]
    Z.f_t = [carve(R1, 57344, 2048, F32)]
    Z.WA = carve(R2, 0, 16384, BF16)
    Z.WB = carve(R3, 0, 16384, BF16)
    Z.exp_row = carve(SM1, 0, 2048, BF16)
    Z.alpha_bf = carve(SM2, 0, 2048, BF16)
    Z.wf_roll = [carve(R3, i * 4096, 2048, BF16) for i in range(2)]
    Z.wf_sgn = [carve(R3, 16384 + i * 4096, 2048, BF16) for i in range(2)]
    Z.wf_sg2 = [carve(R3, 24576 + i * 4096, 2048, BF16) for i in range(2)]
    Z.rows_f32 = carve(R2, 8192, 3 * D, F32)[0:1, :]
    Z.rows_bf = carve(R2, 0, 3 * D, BF16)[0:1, :]
    Z.lnw_bc = carve(R1, 57344, 2048, BF16)
    Z.lnb_bc = carve(R1, 61440, 2048, BF16)
    Z.f0_junk = [carve(R1, 49152 + i * 4096, 2048, BF16) for i in range(2)]
    Z.f_x = [carve(R2, i * 8192, 2048, F32) for i in range(2)]
    Z.bt_row = carve(R2, 24576, 2048, BF16)[0:1, :]
    Z.bt_aux = carve(R2, 28672, 2048, BF16)[0:1, :]
    Z.bt_tmp = Z.f0_junk[0][0:1, :]
    Z.f_ao = [carve(R3, i * 4096, 2048, BF16) for i in range(2)]
    Z.f_u = carve(R3, 8192, 2048, BF16)
    Z.f_junk = carve(R3, 12288, 2048, BF16)
    Z.f_stg = [carve(R3, 16384 + i * 4096, 2048, BF16) for i in range(2)]
    Z.cos_t = carve(SM1, 0, 2048, BF16)
    Z.sinsgn_t = carve(SM2, 0, 2048, BF16)

    Z.psA = es.enter_context(nc.psum_tensor("psA", [128, 2048], F32))
    Z.psB = es.enter_context(nc.psum_tensor("psB", [128, 2048], F32))

    sem_list = ["pre", "cst", "dbg", "wq0", "wq1",
                "wc_kv", "wc_ph", "wc_wo", "wc_wf",
                "a_x0", "a_x1", "a_s", "a_v", "a_pe", "a_ev",
                "b_pe", "b_s", "b_v", "b_sp0", "b_sp1",
                "kv_w", "k_pe", "k_s", "k_v", "v_pe", "v_ev",
                "pw0", "pw1", "p_pe", "p_s", "p_sp0", "p_sp1",
                "c_qg", "c_pe", "c_v", "c_s", "c_dma",
                "d_qk0", "d_qk1", "d_ph0", "d_ph1", "d_pe", "d_v", "wo_w",
                "e_pe", "e_s", "e_sp0", "e_sp1",
                "f0_r", "f0_r2", "f0_w0", "f0_w1", "f0_s", "f0_e", "f0_v", "f0_pe",
                "f_x0", "f_x1", "f_ao0", "f_ao1", "f_v", "f_s", "f_pe",
                "f_p", "f_out0", "f_out1"]
    Z.C = {n: Ctr(es.enter_context(nc.semaphore(n))) for n in sem_list}

    Z.block = es.enter_context(nc.Block())
    with es:
        _emit(Z)
    return nc


def _emit(Z):
    block = Z.block
    C = Z.C
    psA, psB = Z.psA, Z.psB

    @block.sync
    def _(sync):
        C["pre"].inc(sync.dma_start(out=Z.ident[:], in_=Z.ident_h), 16)
        C["pre"].inc(sync.dma_start(out=Z.gamma_sb[:], in_=Z.gamma_h), 16)
        C["pre"].inc(sync.dma_start(out=Z.cos_t, in_=Z.cos_h), 16)
        C["pre"].inc(sync.dma_start(out=Z.sinsgn_t, in_=Z.sin_h), 16)
    PRE = C["pre"].n

    @block.gpsimd
    def _(gp):
        for dst, h, sem in ((Z.WA, 0, "wq0"), (Z.WB, 1, "wq1")):
            for k in range(NCH):
                C[sem].inc(gp.dma_start(
                    out=dst[:, k * 1024:(k + 1) * 1024],
                    in_=Z.wq_h[k * 128:(k + 1) * 128, h * 1024:(h + 1) * 1024]), 16)
        gp.wait_ge(C["a_ev"].h, NCH)
        for src, dst, cs in ((Z.wk_h, Z.wk_bf, "wc_kv"), (Z.wv_h, Z.wv_bf, "wc_kv"),
                             (Z.wphi_h, Z.wphi_bf, "wc_ph"), (Z.wo_h, Z.wo_bf, "wc_wo"),
                             (Z.wffn_h, Z.wf_bf, "wc_wf")):
            for k in range(NCH):
                C[cs].inc(gp.dma_start(out=dst[k * 128:(k + 1) * 128, :],
                                       in_=src[k * 128:(k + 1) * 128, :]), 16)
    WQH = 16 * 16

    a_x_at = []

    @block.sync
    def _(sync):
        for c in range(NCH):
            if c >= 2:
                sync.wait_ge(C["a_v"].h, 2 * (c - 2) + 2)
            a_x_at.append(C["a_x0" if c % 2 == 0 else "a_x1"].inc(sync.dma_start(
                out=Z.xroll[c % 2][:], in_=Z.x_h[c * 128:(c + 1) * 128, :]), 16))

    @block.scalar
    def _(s):
        st = Z.stat
        s.wait_ge(C["cst"].h, 1)
        for c in range(NCH):
            b = 32 * (c % 2)
            s.wait_ge(C["a_x0" if c % 2 == 0 else "a_x1"].h, a_x_at[c])
            C["a_s"].inc(s.activation(Z.slot[c % 2], Z.xroll[c % 2][:], AF.Square,
                                      accum_out=st[:, b + 0:b + 1]))
            s.wait_ge(C["a_v"].h, 2 * c + 1)
            s.activation(st[:, b + 3:b + 4], st[:, b + 6:b + 7], AF.Ln,
                         bias=float(EPS))
            s.drain()
            C["a_s"].inc(s.activation(st[:, b + 4:b + 5], st[:, b + 3:b + 4],
                                      AF.Exp, scale=-0.5))

    @block.vector
    def _(v):
        st = Z.stat
        v.memset(Z.consts[:, 0:1], float(EPS))
        v.memset(Z.consts[:, 1:2], -0.5)
        v.memset(Z.consts[:, 2:3], float(1.0 + EPS))
        v.memset(Z.consts[:, 3:4], float(EPS * EPS))
        C["cst"].inc(v.memset(Z.consts[:, 4:5], 0.0))
        for c in range(NCH):
            b = 32 * (c % 2)
            v.wait_ge(C["a_x0" if c % 2 == 0 else "a_x1"].h, a_x_at[c])
            v.reduce_sum(st[:, b + 7:b + 8], Z.xroll[c % 2][:], axis=AX.X)
            v.drain()
            v.tensor_scalar_mul(st[:, b + 1:b + 2], st[:, b + 7:b + 8], 1.0 / D)
            v.wait_ge(C["a_s"].h, 2 * c + 1)
            v.tensor_scalar_mul(st[:, b + 2:b + 3], st[:, b + 0:b + 1], 1.0 / D)
            v.drain()
            v.tensor_mul(st[:, b + 5:b + 6], st[:, b + 1:b + 2], st[:, b + 1:b + 2])
            v.drain()
            C["a_v"].inc(v.tensor_sub(st[:, b + 6:b + 7], st[:, b + 2:b + 3],
                                      st[:, b + 5:b + 6]))
            v.wait_ge(C["a_s"].h, 2 * c + 2)
            if c >= 2:
                v.wait_ge(C["a_pe"].h, c - 1)
            C["a_v"].inc(v.tensor_scalar(Z.slot[2 + c % 2], Z.xroll[c % 2][:],
                                         st[:, b + 1:b + 2], st[:, b + 4:b + 5],
                                         ALU.subtract, ALU.mult))
            if c >= 1:
                v.wait_ge(C["a_pe"].h, c)
                grp = psA if (c - 1) % 2 == 0 else psB
                outv = Z.xn1.rearrange("p (a b) -> p a b", b=2048)[
                    :, :, (c - 1) * 128:c * 128]
                inv = grp[:].rearrange("p (a b) -> p a b", b=128)
                C["a_ev"].inc(v.tensor_copy(outv, inv))
        v.wait_ge(C["a_pe"].h, NCH)
        grp = psA if (NCH - 1) % 2 == 0 else psB
        outv = Z.xn1.rearrange("p (a b) -> p a b", b=2048)[
            :, :, (NCH - 1) * 128:NCH * 128]
        inv = grp[:].rearrange("p (a b) -> p a b", b=128)
        C["a_ev"].inc(v.tensor_copy(outv, inv))

    @block.tensor
    def _(t):
        t.wait_ge(C["pre"].h, PRE)
        for c in range(NCH):
            t.wait_ge(C["a_v"].h, 2 * c + 2)
            if c >= 2:
                t.wait_ge(C["a_ev"].h, c - 1)
            grp = psA if c % 2 == 0 else psB
            stg = Z.slot[2 + c % 2]
            for i in range(16):
                mm = t.matmul(grp[:, i * 128:(i + 1) * 128],
                              stg[:, i * 128:(i + 1) * 128], Z.ident[:],
                              start=True, stop=True)
            C["a_pe"].inc(mm)

    if DEBUG:
        @block.sync
        def _(sync):
            sync.wait_ge(C["a_ev"].h, NCH)
            C["dbg"].inc(sync.dma_start(out=Z.dbg["dbg_xn1"], in_=Z.xn1), 16)
    if MAXPH < 2:
        _tail(block, C, [("a_ev", NCH)])
        return

    @block.tensor
    def _(t):
        t.wait_ge(C["a_ev"].h, NCH)
        for oc in range(NCH):
            t.wait_ge(C["wq0" if oc < 8 else "wq1"].h, WQH)
            if oc >= 2:
                t.wait_ge(C["b_s"].h, 2 * (oc - 2) + 1)
            grp = psA if oc % 2 == 0 else psB
            for k in range(NCH):
                if oc < 8:
                    wt = Z.WA[:, k * 1024 + oc * 128:k * 1024 + oc * 128 + 128]
                else:
                    wt = Z.WB[:, k * 1024 + (oc - 8) * 128:k * 1024 + (oc - 8) * 128 + 128]
                for j in range(4):
                    mm = t.matmul(grp[:, j * 512:(j + 1) * 512], wt,
                                  Z.xn1[:, k * 2048 + j * 512:k * 2048 + (j + 1) * 512],
                                  start=(k == 0), stop=(k == NCH - 1))
            C["b_pe"].inc(mm)

    @block.scalar
    def _(s):
        for oc in range(NCH):
            sl = Z.slot if oc % 2 == 0 else Z.slot2
            grp = psA if oc % 2 == 0 else psB
            s.wait_ge(C["b_pe"].h, oc + 1)
            if oc >= 2:
                s.wait_ge(C["b_v"].h, 3 * (oc - 2) + 3)
            C["b_s"].inc(s.activation(sl[0], grp[:], AF.Copy))
            s.wait_ge(C["b_v"].h, 3 * oc + 2)
            C["b_s"].inc(s.activation(sl[2], sl[3], AF.Exp))

    @block.vector
    def _(v):
        for oc in range(NCH):
            sl = Z.slot if oc % 2 == 0 else Z.slot2
            v.wait_ge(C["b_s"].h, 2 * oc + 1)
            v.tensor_copy(sl[1][0:32, :], sl[0][32:64, :])
            v.tensor_copy(sl[1][32:64, :], sl[0][0:32, :])
            v.tensor_copy(sl[1][64:96, :], sl[0][96:128, :])
            v.tensor_copy(sl[1][96:128, :], sl[0][64:96, :])
            v.drain()
            v.tensor_mul(sl[3], sl[1], Z.sinsgn_t)
            v.tensor_mul(sl[4], sl[0], Z.cos_t)
            v.drain()
            C["b_v"].inc(v.tensor_add(sl[1], sl[3], sl[4]))
            v.drain()
            v.tensor_scalar_min(sl[3], sl[1], 0.0)
            C["b_v"].inc(v.tensor_scalar_max(sl[4], sl[1], 0.0))
            v.wait_ge(C["b_s"].h, 2 * oc + 2)
            if oc >= 2:
                v.wait_ge(C["b_sp0" if oc % 2 == 0 else "b_sp1"].h, 16 * (oc // 2))
            C["b_v"].inc(v.scalar_tensor_tensor(
                Z.qkspl[oc % 2], sl[2], 1.0, sl[4], ALU.mult, ALU.add,
                accum_out=Z.qg_sb[:, oc:oc + 1]))

    @block.sync
    def _(sync):
        for oc in range(NCH):
            sync.wait_ge(C["b_v"].h, 3 * oc + 3)
            C["b_sp0" if oc % 2 == 0 else "b_sp1"].inc(
                sync.dma_start(out=Z.qk_sp[oc], in_=Z.qkspl[oc % 2]), 16)

    if MAXPH < 3:
        _tail(block, C)
        return

    @block.vector
    def _(v):
        v.wait_ge(C["b_v"].h, 3 * NCH)
        v.drain()
        v.tensor_scalar_mul(Z.qg_bf[:], Z.qg_sb[:], 1.0 / S)
        v.drain()
        C["c_qg"].inc(v.engine_nop())



    @block.vector
    def _(v):
        v.wait_ge(C["c_qg"].h, 2)
        v.tensor_copy(Z.qg_tr[:], psA[0:16, 0:128])
        v.drain()
        C["c_qg"].inc(v.engine_nop())
        v.memset(Z.qg_arr[:], 0.0)
        v.drain()
        C["c_qg"].inc(v.engine_nop())

    @block.sync
    def _(sync):
        sync.wait_ge(C["c_qg"].h, 4)
        C["c_dma"].inc(sync.dma_start(out=Z.qg_dram, in_=Z.qg_tr[:]), 16)
        sync.wait_ge(C["c_dma"].h, 16)
        qgd = Z.qg_dram.rearrange("a b -> (a b)")
        for n in range(NH):
            g = n // 4
            ch = (g * 64) // 128
            pb = (g * 64) % 128
            C["c_dma"].inc(sync.dma_start(
                out=Z.qg_arr[pb:pb + 64, ch * 32 + n:ch * 32 + n + 1],
                in_=qgd[n * 64:(n + 1) * 64].rearrange("(a b) -> a b", b=1)), 16)
    QGA = C["c_dma"].n

    @block.gpsimd
    def _(gp):
        gp.wait_ge(C["b_pe"].h, 8)
        gp.wait_ge(C["wc_kv"].h, 32 * 16)
        for k in range(NCH):
            C["kv_w"].inc(gp.dma_start(out=Z.WA[:, k * 512:(k + 1) * 512],
                                       in_=Z.wk_bf[k * 128:(k + 1) * 128, :]), 16)
        for k in range(NCH):
            C["kv_w"].inc(gp.dma_start(
                out=Z.WA[:, 8192 + k * 512:8192 + (k + 1) * 512],
                in_=Z.wv_bf[k * 128:(k + 1) * 128, :]), 16)
    KVW = 32 * 16

    @block.tensor
    def _(t):
        t.wait_ge(C["kv_w"].h, KVW)
        t.wait_ge(C["b_v"].h, 3 * NCH)
        t.wait_ge(C["b_s"].h, 2 * NCH)
        t.wait_ge(C["b_sp0"].h, 16 * 8)
        t.wait_ge(C["b_sp1"].h, 16 * 8)
        for kc in range(4):
            if kc == 2:
                t.wait_ge(C["c_qg"].h, 1)
                t.wait_ge(C["k_s"].h, 1)
                C["c_qg"].inc(t.matmul(psA[0:16, 0:128], Z.qg_bf[:], Z.ident[:],
                                       start=True, stop=True))
            if kc >= 2:
                t.wait_ge(C["k_s"].h, 2 * (kc - 2) + 1)
                if kc == 2:
                    t.wait_ge(C["c_qg"].h, 3)
            grp = psA if kc % 2 == 0 else psB
            for k in range(NCH):
                wt = Z.WA[:, k * 512 + kc * 128:k * 512 + kc * 128 + 128]
                for j in range(4):
                    mm = t.matmul(grp[:, j * 512:(j + 1) * 512], wt,
                                  Z.xn1[:, k * 2048 + j * 512:k * 2048 + (j + 1) * 512],
                                  start=(k == 0), stop=(k == NCH - 1))
            C["k_pe"].inc(mm)

    @block.scalar
    def _(s):
        for kc in range(4):
            sl = Z.slot if kc % 2 == 0 else Z.slot2
            grp = psA if kc % 2 == 0 else psB
            s.wait_ge(C["k_pe"].h, kc + 1)
            C["k_s"].inc(s.activation(sl[0], grp[:], AF.Copy))
            s.wait_ge(C["k_v"].h, 3 * kc + 2)
            C["k_s"].inc(s.activation(sl[2], sl[3], AF.Exp))

    @block.vector
    def _(v):
        for kc in range(4):
            sl = Z.slot if kc % 2 == 0 else Z.slot2
            v.wait_ge(C["k_s"].h, 2 * kc + 1)
            v.tensor_copy(sl[1][0:32, :], sl[0][32:64, :])
            v.tensor_copy(sl[1][32:64, :], sl[0][0:32, :])
            v.tensor_copy(sl[1][64:96, :], sl[0][96:128, :])
            v.tensor_copy(sl[1][96:128, :], sl[0][64:96, :])
            v.drain()
            v.tensor_mul(sl[3], sl[1], Z.sinsgn_t)
            v.tensor_mul(sl[4], sl[0], Z.cos_t)
            v.drain()
            C["k_v"].inc(v.tensor_add(sl[1], sl[3], sl[4]))
            v.drain()
            v.tensor_scalar_min(sl[3], sl[1], 0.0)
            C["k_v"].inc(v.tensor_scalar_max(sl[4], sl[1], 0.0))
            v.wait_ge(C["k_s"].h, 2 * kc + 2)
            C["k_v"].inc(v.scalar_tensor_tensor(
                Z.kkf[:, kc * 2048:(kc + 1) * 2048], sl[2], 1.0, sl[4],
                ALU.mult, ALU.add))

    @block.tensor
    def _(t):
        t.wait_ge(C["k_s"].h, 8)
        t.wait_ge(C["k_v"].h, 12)
        for sc in range(NCH):
            if sc >= 8:
                t.wait_ge(C["v_ev"].h, sc - 8 + 1)
            grp = psA if (sc % 8) < 4 else psB
            bs = (sc % 4) * 512
            for k in range(NCH):
                mm = t.matmul(grp[:, bs:bs + 512],
                              Z.xn1[:, k * 2048 + sc * 128:k * 2048 + sc * 128 + 128],
                              Z.WA[:, 8192 + k * 512:8192 + (k + 1) * 512],
                              start=(k == 0), stop=(k == NCH - 1))
            C["v_pe"].inc(mm)

    @block.vector
    def _(v):
        for sc in range(NCH):
            v.wait_ge(C["v_pe"].h, sc + 1)
            grp = psA if (sc % 8) < 4 else psB
            bs = (sc % 4) * 512
            C["v_ev"].inc(v.tensor_copy(Z.vt[:, sc * 512:(sc + 1) * 512],
                                        grp[:, bs:bs + 512]))

    @block.gpsimd
    def _(gp):
        gp.wait_ge(C["b_pe"].h, NCH)
        gp.wait_ge(C["wc_ph"].h, 16 * 16)
        for k in range(NCH):
            C["pw0"].inc(gp.dma_start(
                out=Z.WB[:, k * 1024:(k + 1) * 1024],
                in_=Z.wphi_bf[k * 128:(k + 1) * 128, 0:1024]), 16)
        gp.wait_ge(C["v_pe"].h, NCH)
        for k in range(NCH):
            C["pw1"].inc(gp.dma_start(
                out=Z.WA[:, k * 1024:(k + 1) * 1024],
                in_=Z.wphi_bf[k * 128:(k + 1) * 128, 1024:2048]), 16)
    PWH = 16 * 16

    @block.tensor
    def _(t):
        t.wait_ge(C["c_dma"].h, QGA)
        t.wait_ge(C["k_v"].h, 12)
        t.wait_ge(C["v_ev"].h, NCH)
        for j in range(4):
            for g2 in range(4):
                mm = t.matmul(psB[0:32, j * 512:(j + 1) * 512],
                              Z.qg_arr[:, g2 * 32:(g2 + 1) * 32],
                              Z.kkf[:, g2 * 2048 + j * 512:g2 * 2048 + (j + 1) * 512],
                              start=(g2 == 0), stop=(g2 == 3))
        C["c_pe"].inc(mm)
        for kc in range(4):
            if kc >= 1:
                t.wait_ge(C["c_v"].h, 2 + kc)
            for j in range(16):
                mm = t.matmul(psA[:, j * 128:(j + 1) * 128],
                              Z.kkf[:, kc * 2048 + j * 128:kc * 2048 + j * 128 + 128],
                              Z.ident[:], start=True, stop=True)
            C["c_pe"].inc(mm)
        t.wait_ge(C["c_v"].h, 2)
        t.wait_ge(C["c_s"].h, 1)
        for q4 in range(4):
            if q4 >= 1:
                t.wait_ge(C["c_v"].h, 6 + q4)
            for i in range(4):
                sc = q4 * 4 + i
                mm = t.matmul(psB[:, i * 512:i * 512 + 32],
                              Z.alpha_bf[0:32, sc * 128:(sc + 1) * 128],
                              Z.ident[0:32, 0:32], start=True, stop=True)
            C["c_pe"].inc(mm)

    @block.tensor
    def _(t):
        t.wait_ge(C["v_ev"].h, NCH)
        t.wait_ge(C["c_pe"].h, 9)
        t.wait_ge(C["c_v"].h, 10)
        for oc in range(NCH):
            t.wait_ge(C["pw0" if oc < 8 else "pw1"].h, PWH)
            if oc >= 2:
                t.wait_ge(C["p_s"].h, oc - 1)
            grp = psA if oc % 2 == 0 else psB
            for k in range(NCH):
                if oc < 8:
                    wt = Z.WB[:, k * 1024 + oc * 128:k * 1024 + oc * 128 + 128]
                else:
                    wt = Z.WA[:, k * 1024 + (oc - 8) * 128:k * 1024 + (oc - 8) * 128 + 128]
                for j in range(4):
                    mm = t.matmul(grp[:, j * 512:(j + 1) * 512], wt,
                                  Z.xn1[:, k * 2048 + j * 512:k * 2048 + (j + 1) * 512],
                                  start=(k == 0), stop=(k == NCH - 1))
            C["p_pe"].inc(mm)

    @block.scalar
    def _(s):
        s.wait_ge(C["c_v"].h, 1)
        C["c_s"].inc(s.activation(Z.exp_row[0:32, :], psB[0:32, :], AF.Exp,
                                  bias=Z.sm_cols[:, 0:1],
                                  accum_out=Z.sm_cols[:, 1:2]))

    @block.scalar
    def _(s):
        for oc in range(NCH):
            s.wait_ge(C["p_pe"].h, oc + 1)
            if oc >= 2:
                s.wait_ge(C["p_sp0" if oc % 2 == 0 else "p_sp1"].h, 16 * (oc // 2))
            grp = psA if oc % 2 == 0 else psB
            C["p_s"].inc(s.activation(Z.phspl[oc % 2], grp[:], AF.Copy))

    @block.sync
    def _(sync):
        for oc in range(NCH):
            sync.wait_ge(C["p_s"].h, oc + 1)
            C["p_sp0" if oc % 2 == 0 else "p_sp1"].inc(
                sync.dma_start(out=Z.phi_sp[oc], in_=Z.phspl[oc % 2]), 16)

    if MAXPH < 4:
        _tail(block, C)
        return

    @block.tensor
    def _(t):
        t.wait_ge(C["p_s"].h, NCH)
        for sc in range(NCH):
            t.wait_ge(C["c_v"].h, 12 + sc)
            for c2 in range(NCH):
                mm = t.matmul(psA[:, c2 * 128:c2 * 128 + 64],
                              Z.wks[sc % 6][:, c2 * 128:(c2 + 1) * 128],
                              Z.vt[:, sc * 512 + (c2 // 2) * 64:
                                   sc * 512 + (c2 // 2) * 64 + 64],
                              start=(sc == 0 and c2 % 4 == 0),
                              stop=(sc == NCH - 1 and c2 % 4 == 3))
            C["c_pe"].inc(mm)


    @block.vector
    def _(v):
        v.wait_ge(C["c_pe"].h, 1)
        v.reduce_max(Z.sm_cols[:, 0:1], psB[0:32, :], axis=AX.X, negate=True)
        v.drain()
        C["c_v"].inc(v.engine_nop())
        v.wait_ge(C["c_s"].h, 1)
        v.reciprocal(Z.sm_cols[:, 2:3], Z.sm_cols[:, 1:2])
        v.drain()
        v.tensor_scalar_mul(Z.sm_cols[:, 2:3], Z.sm_cols[:, 2:3], float(S))
        v.drain()
        v.tensor_scalar(Z.alpha_bf[0:32, :], Z.exp_row[0:32, :],
                        Z.sm_cols[:, 2:3], None, ALU.mult)
        v.drain()
        C["c_v"].inc(v.engine_nop())
        for kc in range(4):
            v.wait_ge(C["c_pe"].h, 2 + kc)
            outv = Z.kkt.rearrange("p (a b) -> p a b", b=512)[:, :, kc * 128:kc * 128 + 128]
            inv = psA[:].rearrange("p (a b) -> p a b", b=128)
            C["c_v"].inc(v.tensor_copy(outv, inv))
        for q4 in range(4):
            v.wait_ge(C["c_pe"].h, 6 + q4)
            outv = Z.alpha_tm.rearrange("p (a b) -> p a b", b=32)[:, q4 * 4:(q4 + 1) * 4, :]
            inv = psB[:].rearrange("p (a b) -> p a b", b=512)[:, :, 0:32]
            C["c_v"].inc(v.tensor_copy(outv, inv))
        v.wait_ge(C["c_v"].h, 10)
        v.drain()
        C["c_v"].inc(v.memset(Z.bd_sb[:], 0.0))
        for sc in range(NCH):
            if sc >= 6:
                v.wait_ge(C["c_pe"].h, 4 + sc)
            in0 = Z.kkt[:, sc * 512:(sc + 1) * 512].rearrange(
                "p (g j) -> p g j", g=8)
            in0 = in0.unsqueeze(2).broadcast_to([128, 8, 4, 64])
            in1 = Z.alpha_tm[:, sc * 32:(sc + 1) * 32].rearrange(
                "p (g n) -> p g n", g=8)
            in1 = in1.unsqueeze(3).broadcast_to([128, 8, 4, 64])
            outv = Z.wks[sc % 6][:].rearrange("p (g n j) -> p g n j", g=8, n=4)
            C["c_v"].inc(v.tensor_tensor(outv, in0, in1, ALU.mult))
        v.wait_ge(C["c_pe"].h, 9 + NCH)
        for c2 in range(NCH):
            v.tensor_copy(Z.bd_sb[0:64, c2 * 128:c2 * 128 + 64],
                          psA[0:64, c2 * 128:c2 * 128 + 64])
            C["c_v"].inc(v.tensor_copy(
                Z.bd_sb[64:128, c2 * 128 + 64:c2 * 128 + 128],
                psA[64:128, c2 * 128:c2 * 128 + 64]))
    CV_END = 43

    if DEBUG:
        @block.sync
        def _(sync):
            sync.wait_ge(C["c_v"].h, CV_END)
            C["dbg"].inc(sync.dma_start(out=Z.dbg["dbg_kkt"], in_=Z.kkt), 16)
            C["dbg"].inc(sync.dma_start(out=Z.dbg["dbg_kkf"], in_=Z.kkf), 16)
            C["dbg"].inc(sync.dma_start(out=Z.dbg["dbg_vt"], in_=Z.vt), 16)
            C["dbg"].inc(sync.dma_start(out=Z.dbg["dbg_alpha"], in_=Z.alpha_tm[:]), 16)
            C["dbg"].inc(sync.dma_start(out=Z.dbg["dbg_qgarr"], in_=Z.qg_arr[:]), 16)
            C["dbg"].inc(sync.dma_start(out=Z.dbg["dbg_bd"], in_=Z.bd_sb[:]), 16)
    if MAXPH < 5:
        _tail(block, C, [("c_v", CV_END)])
        return

    @block.sync
    def _(sync):
        sync.wait_ge(C["p_pe"].h, NCH)
        for c2 in range(NCH):
            if c2 >= 2:
                sync.wait_ge(C["d_pe"].h, c2 - 1)
            C["d_qk0" if c2 % 2 == 0 else "d_qk1"].inc(
                sync.dma_start(out=Z.qkroll[c2 % 2], in_=Z.qk_sp[c2]), 16)

    @block.scalar
    def _(s):
        s.wait_ge(C["p_pe"].h, NCH)
        s.wait_ge(C["p_sp0"].h, 16 * 8)
        s.wait_ge(C["p_sp1"].h, 16 * 8)
        for c2 in range(NCH):
            if c2 >= 2:
                s.wait_ge(C["d_v"].h, 2 * (c2 - 2) + 2)
            C["d_ph0" if c2 % 2 == 0 else "d_ph1"].inc(
                s.dma_start(out=Z.phiroll[c2 % 2], in_=Z.phi_sp[c2]), 16)

    @block.tensor
    def _(t):
        t.wait_ge(C["c_v"].h, CV_END)
        for c2 in range(NCH):
            t.wait_ge(C["d_qk0" if c2 % 2 == 0 else "d_qk1"].h, 16 * (c2 // 2 + 1))
            if c2 >= 2:
                t.wait_ge(C["d_v"].h, 2 * (c2 - 2) + 1)
            grp = psA if c2 % 2 == 0 else psB
            for j in range(4):
                mm = t.matmul(grp[:, j * 512:(j + 1) * 512],
                              Z.bd_sb[:, c2 * 128:(c2 + 1) * 128],
                              Z.qkroll[c2 % 2][:, j * 512:(j + 1) * 512],
                              start=True, stop=True)
            C["d_pe"].inc(mm)

    @block.vector
    def _(v):
        for c2 in range(NCH):
            grp = psA if c2 % 2 == 0 else psB
            v.wait_ge(C["d_pe"].h, c2 + 1)
            C["d_v"].inc(v.tensor_copy(Z.ra_ev[c2 % 2], grp[:]))
            v.drain()
            v.wait_ge(C["d_ph0" if c2 % 2 == 0 else "d_ph1"].h, 16 * (c2 // 2 + 1))
            C["d_v"].inc(v.tensor_mul(Z.ctx[:, c2 * 2048:(c2 + 1) * 2048],
                                      Z.phiroll[c2 % 2], Z.ra_ev[c2 % 2]))

    if DEBUG:
        @block.sync
        def _(sync):
            sync.wait_ge(C["d_v"].h, 2 * NCH)
            C["dbg"].inc(sync.dma_start(out=Z.dbg["dbg_ctx"], in_=Z.ctx), 16)
    if MAXPH < 6:
        _tail(block, C, [("d_v", 2 * NCH)])
        return

    @block.gpsimd
    def _(gp):
        gp.wait_ge(C["p_pe"].h, 8)
        gp.wait_ge(C["wc_wo"].h, 16 * 16)
        for k in range(NCH):
            C["wo_w"].inc(gp.dma_start(out=Z.WB[:, k * 1024:(k + 1) * 1024],
                                       in_=Z.wo_bf[k * 128:(k + 1) * 128, 0:1024]), 16)
        gp.wait_ge(C["p_pe"].h, NCH)
        gp.wait_ge(C["c_pe"].h, 9)
        for k in range(NCH):
            C["wo_w"].inc(gp.dma_start(out=Z.WA[:, k * 1024:(k + 1) * 1024],
                                       in_=Z.wo_bf[k * 128:(k + 1) * 128, 1024:2048]), 16)
    WO = 32 * 16

    @block.tensor
    def _(t):
        t.wait_ge(C["wo_w"].h, WO)
        t.wait_ge(C["d_v"].h, 2 * NCH)
        for sc in range(NCH):
            if sc >= 2:
                t.wait_ge(C["e_s"].h, sc - 1)
            grp = psA if sc % 2 == 0 else psB
            for k in range(NCH):
                lh = Z.ctx[:, k * 2048 + sc * 128:k * 2048 + sc * 128 + 128]
                for nt in range(4):
                    if nt < 2:
                        rhs = Z.WB[:, k * 1024 + nt * 512:k * 1024 + (nt + 1) * 512]
                    else:
                        rhs = Z.WA[:, k * 1024 + (nt - 2) * 512:k * 1024 + (nt - 1) * 512]
                    mm = t.matmul(grp[:, nt * 512:(nt + 1) * 512], lh, rhs,
                                  start=(k == 0), stop=(k == NCH - 1))
            C["e_pe"].inc(mm)

    @block.scalar
    def _(s):
        for sc in range(NCH):
            s.wait_ge(C["e_pe"].h, sc + 1)
            if sc >= 2:
                s.wait_ge(C["e_sp0" if sc % 2 == 0 else "e_sp1"].h, 16 * (sc // 2))
            grp = psA if sc % 2 == 0 else psB
            C["e_s"].inc(s.activation(Z.ao_ev[sc % 2], grp[:], AF.Copy))

    @block.sync
    def _(sync):
        for sc in range(NCH):
            sync.wait_ge(C["e_s"].h, sc + 1)
            C["e_sp0" if sc % 2 == 0 else "e_sp1"].inc(
                sync.dma_start(out=Z.ao_sp[sc], in_=Z.ao_ev[sc % 2]), 16)

    if MAXPH < 7:
        _tail(block, C)
        return

    @block.sync
    def _(sync):
        sync.wait_ge(C["e_sp0"].h, 16 * 8)
        sync.wait_ge(C["e_sp1"].h, 16 * 8)
        C["f0_r"].inc(sync.dma_start(out=Z.rows_f32, in_=Z.rows3_h), 16)
        sync.wait_ge(C["wc_wf"].h, 16 * 16)
        for oc in range(NCH):
            if oc >= 2:
                sync.wait_ge(C["f0_s"].h, oc - 1)
                sync.wait_ge(C["f0_v"].h, 5 + 3 * (oc - 2))
            C["f0_w0" if oc % 2 == 0 else "f0_w1"].inc(
                sync.dma_start(out=Z.wf_roll[oc % 2][:],
                               in_=Z.wf_bf[oc * 128:(oc + 1) * 128, :]), 16)
        sync.wait_ge(C["f0_v"].h, 56)
        C["f0_r2"].inc(sync.dma_start(
            out=Z.row_sp.rearrange("o (a b) -> (o a) b", a=16),
            in_=Z.rowl_ev[:]), 16)
        sync.wait_ge(C["f0_r2"].h, 16)
        C["f0_r2"].inc(sync.dma_start(out=Z.bt_aux[:], in_=Z.row_sp), 16)

    @block.scalar
    def _(s):
        for oc in range(NCH):
            s.wait_ge(C["f0_w0" if oc % 2 == 0 else "f0_w1"].h, 16 * (oc // 2 + 1))
            if oc >= 2:
                s.wait_ge(C["f0_v"].h, 7 + 3 * (oc - 2))
            C["f0_s"].inc(s.activation(Z.wf_sgn[oc % 2], Z.wf_roll[oc % 2][:],
                                       AF.Sign))
            if oc >= 1:
                j = oc - 1
                s.wait_ge(C["f0_pe"].h, j + 3)
                grp = psA if j % 2 == 0 else psB
                outv = Z.qwt.rearrange("p (a b) -> p a b",
                                       b=2048)[:, :, j * 128:(j + 1) * 128]
                inv = grp[:].rearrange("p (a b) -> p a b", b=128)
                C["f0_e"].inc(s.activation(outv, inv, AF.Copy))
        s.wait_ge(C["f0_pe"].h, NCH + 2)
        j = NCH - 1
        grp = psA if j % 2 == 0 else psB
        outv = Z.qwt.rearrange("p (a b) -> p a b",
                               b=2048)[:, :, j * 128:(j + 1) * 128]
        inv = grp[:].rearrange("p (a b) -> p a b", b=128)
        C["f0_e"].inc(s.activation(outv, inv, AF.Copy))

    @block.vector
    def _(v):
        v.wait_ge(C["e_s"].h, NCH)
        v.wait_ge(C["f0_r"].h, 16)
        v.tensor_copy(Z.rows_bf, Z.rows_f32)
        v.drain()
        C["f0_v"].inc(v.engine_nop())
        C["f0_v"].inc(v.memset(Z.ones_bf[:], 1.0))
        v.wait_ge(C["f0_pe"].h, 2)
        C["f0_v"].inc(v.tensor_copy(Z.lnw_bc, psA[:]))
        C["f0_v"].inc(v.tensor_copy(Z.lnb_bc, psB[:]))
        v.drain()
        for oc in range(NCH):
            v.wait_ge(C["f0_w0" if oc % 2 == 0 else "f0_w1"].h, 16 * (oc // 2 + 1))
            C["f0_v"].inc(v.tensor_reduce(
                Z.maxw_sb[:, oc:oc + 1], Z.wf_roll[oc % 2][:], axis=AX.X,
                op=ALU.max, apply_absolute_value=True))
            v.wait_ge(C["f0_s"].h, oc + 1)
            if oc >= 2:
                v.wait_ge(C["f0_pe"].h, oc + 1)
            C["f0_v"].inc(v.scalar_tensor_tensor(
                Z.wf_sg2[oc % 2], Z.wf_sgn[oc % 2], Z.gamma_sb[:, oc:oc + 1],
                Z.lnw_bc, ALU.mult, ALU.mult))
            C["f0_v"].inc(v.scalar_tensor_tensor(
                Z.f0_junk[oc % 2], Z.wf_sgn[oc % 2], Z.gamma_sb[:, oc:oc + 1],
                Z.lnb_bc, ALU.mult, ALU.mult,
                accum_out=Z.coll_sb[:, oc:oc + 1]))
        v.drain()
        v.tensor_reduce(Z.maxw_sb[:, 16:17], Z.maxw_sb[:, 0:16],
                        axis=AX.X, op=ALU.max)
        v.drain()
        v.tensor_copy(Z.maxw_bf[:], Z.maxw_sb[:, 16:17])
        v.drain()
        C["f0_v"].inc(v.engine_nop())
        v.tensor_copy(Z.coll_bf[:], Z.coll_sb[:])
        v.drain()
        C["f0_v"].inc(v.engine_nop())
        v.wait_ge(C["f0_pe"].h, NCH + 3)
        v.tensor_reduce(Z.scale_col[0:1, 0:1], psB[0:1, 0:128], axis=AX.X,
                        op=ALU.max)
        v.drain()
        v.reciprocal(Z.maxw_sb[0:1, 17:18], Z.scale_col[0:1, 0:1])
        v.drain()
        C["f0_v"].inc(v.engine_nop())
        v.wait_ge(C["f0_pe"].h, NCH + 4)
        C["f0_v"].inc(v.tensor_copy(Z.rowl_ev[:], psA[0:16, 0:128]))
        v.wait_ge(C["f0_r2"].h, 32)
        v.tensor_scalar(Z.bt_tmp[:], Z.rows_bf[:, 4096:6144],
                        Z.maxw_sb[0:1, 17:18], None, ALU.mult)
        v.drain()
        C["f0_v"].inc(v.tensor_add(Z.bt_row[:], Z.bt_tmp[:],
                                   Z.bt_aux[:]))
    FV_MX = 53
    FV_BETA = 57

    @block.tensor
    def _(t):
        t.wait_ge(C["e_s"].h, NCH)
        t.wait_ge(C["f0_v"].h, 2)
        for nt in range(4):
            mm = t.matmul(psA[:, nt * 512:(nt + 1) * 512], Z.ones_bf[:],
                          Z.rows_bf[:, nt * 512:(nt + 1) * 512],
                          start=True, stop=True)
        C["f0_pe"].inc(mm)
        for nt in range(4):
            mm = t.matmul(psB[:, nt * 512:(nt + 1) * 512], Z.ones_bf[:],
                          Z.rows_bf[:, 2048 + nt * 512:2048 + (nt + 1) * 512],
                          start=True, stop=True)
        C["f0_pe"].inc(mm)

    @block.tensor
    def _(t):
        for oc in range(NCH):
            t.wait_ge(C["f0_v"].h, 6 + 3 * oc)
            if oc >= 2:
                t.wait_ge(C["f0_e"].h, oc - 1)
            grp = psA if oc % 2 == 0 else psB
            for kc in range(16):
                mm = t.matmul(grp[:, kc * 128:(kc + 1) * 128],
                              Z.wf_sg2[oc % 2][:, kc * 128:(kc + 1) * 128],
                              Z.ident[:], start=True, stop=True)
            C["f0_pe"].inc(mm)

    @block.tensor
    def _(t):
        t.wait_ge(C["f0_v"].h, FV_MX)
        t.wait_ge(C["f0_e"].h, 16)
        C["f0_pe"].inc(t.matmul(psB[0:1, 0:128], Z.maxw_bf[:], Z.ident[:],
                                start=True, stop=True))
        t.wait_ge(C["f0_v"].h, 54)
        C["f0_pe"].inc(t.matmul(psA[0:16, 0:128], Z.coll_bf[:],
                                Z.ident[:], start=True, stop=True))
    PE_BETA = NCH + 4

    if DEBUG:
        @block.sync
        def _(sync):
            sync.wait_ge(C["f0_v"].h, FV_BETA)
            C["dbg"].inc(sync.dma_start(out=Z.dbg["dbg_qwt"], in_=Z.qwt), 16)
            C["dbg"].inc(sync.dma_start(out=Z.dbg["dbg_scale"],
                                        in_=Z.scale_col[:, 0:1]), 16)
    if MAXPH < 8:
        _tail(block, C, [("f0_v", FV_BETA)])
        return

    fvV = {k: {} for k in range(1, 10)}
    fvG = {k: {} for k in (11, 12, 13)}
    pos = 0
    for k in range(1, 9):
        pos += 1; fvV[k][0] = pos
    for sc in range(NCH):
        pos += 1; fvV[9][sc] = pos
        if sc + 1 < NCH:
            for k in range(1, 9):
                pos += 1; fvV[k][sc + 1] = pos
        if sc >= 1:
            for k in (11, 12, 13):
                pos += 1; fvG[k][sc - 1] = pos
    for k in (11, 12, 13):
        pos += 1; fvG[k][NCH - 1] = pos
    fsS = {k: {} for k in (2, 3, 5, 6, 7, 8, 10)}
    pos = 0
    for k in (2, 3, 5):
        pos += 1; fsS[k][0] = pos
    for sc in range(NCH):
        pos += 1; fsS[10][sc] = pos
        if sc + 1 < NCH:
            for k in (2, 3, 5):
                pos += 1; fsS[k][sc + 1] = pos
        if sc >= 1:
            for k in (6, 7, 8):
                pos += 1; fsS[k][sc - 1] = pos
    for k in (6, 7, 8):
        pos += 1; fsS[k][NCH - 1] = pos
    fp_P2 = {}
    pos = 0
    for sc in range(NCH):
        pos += 1; fp_P2[sc] = pos

    def pe_T(sc):
        return 2 * sc + 1

    def pe_F3(sc):
        return 2 * sc + 2

    f_x_at, f_ao_at = [], []

    @block.sync
    def _(sync):
        sync.wait_ge(C["f0_pe"].h, PE_BETA)
        sync.wait_ge(C["f0_v"].h, FV_BETA)
        sync.wait_ge(C["f0_s"].h, NCH)
        for sc in range(NCH):
            if sc >= 2:
                sync.wait_ge(C["f_v"].h, fvV[3][sc - 2])
                sync.wait_ge(C["f_s"].h, fsS[2][sc - 2])
            f_ao_at.append(C["f_ao0" if sc % 2 == 0 else "f_ao1"].inc(
                sync.dma_start(out=Z.f_ao[sc % 2][:], in_=Z.ao_sp[sc]), 16))
            if sc >= 2:
                sync.wait_ge(C["f_v"].h, fvV[4][sc - 2])
            f_x_at.append(C["f_x0" if sc % 2 == 0 else "f_x1"].inc(sync.dma_start(
                out=Z.f_x[sc % 2][:], in_=Z.x_h[sc * 128:(sc + 1) * 128, :]), 16))
        sync.wait_ge(C["f_out0"].h, 16 * 8)
        sync.wait_ge(C["f_out1"].h, 16 * 8)

    @block.scalar
    def _(s):
        st = Z.stat
        s.wait_ge(C["e_sp0"].h, 16 * 8)
        s.wait_ge(C["e_sp1"].h, 16 * 8)
        s.wait_ge(C["f0_v"].h, FV_MX)

        def s_f1(s, sc):
            b = 32 * (sc % 2)
            s.wait_ge(C["f_ao0" if sc % 2 == 0 else "f_ao1"].h, f_ao_at[sc])
            C["f_s"].inc(s.activation(Z.f_junk, Z.f_ao[sc % 2][:], AF.Square,
                                      accum_out=st[:, b + 1:b + 2]))
            s.wait_ge(C["f_v"].h, fvV[2][sc])
            s.activation(st[:, b + 11:b + 12], st[:, b + 4:b + 5], AF.Ln,
                         bias=float(EPS))
            s.drain()
            C["f_s"].inc(s.activation(Z.r2_sb[:, sc:sc + 1], st[:, b + 11:b + 12],
                                      AF.Exp, scale=-0.5))
            s.wait_ge(C["f_v"].h, fvV[7][sc])
            s.activation(st[:, b + 11:b + 12], st[:, b + 10:b + 11], AF.Ln,
                         scale=float(1.0 + EPS), bias=float(EPS * EPS))
            s.drain()
            C["f_s"].inc(s.activation(st[:, b + 12:b + 13], st[:, b + 11:b + 12],
                                      AF.Exp, scale=-0.5))

        def s_evac(s, sc):
            s.wait_ge(C["f_pe"].h, pe_T(sc))
            grp = psA if sc % 2 == 0 else psB
            C["f_s"].inc(s.activation(Z.xn2w[sc % 2][:, 1024:2048],
                                      grp[:, 1024:2048], AF.Copy))

        def s_gtail(s, j):
            b = 32 * (j % 2)
            grp = psA if j % 2 == 0 else psB
            s.wait_ge(C["f_pe"].h, pe_F3(j))
            C["f_s"].inc(s.activation(Z.f_junk, grp[:], AF.Copy,
                                      accum_out=st[:, b + 13:b + 14]))
            s.drain()
            C["f_s"].inc(s.activation(Z.f_junk, grp[:], AF.Square,
                                      accum_out=st[:, b + 14:b + 15]))
            s.wait_ge(C["f_v"].h, fvG[12][j])
            s.activation(st[:, b + 18:b + 19], st[:, b + 17:b + 18], AF.Ln,
                         bias=float(EPS))
            s.drain()
            C["f_s"].inc(s.activation(st[:, b + 19:b + 20], st[:, b + 18:b + 19],
                                      AF.Exp, scale=-0.5))

        s_f1(s, 0)
        for sc in range(NCH):
            s_evac(s, sc)
            if sc + 1 < NCH:
                s_f1(s, sc + 1)
            if sc >= 1:
                s_gtail(s, sc - 1)
        s_gtail(s, NCH - 1)

    @block.gpsimd
    def _(gp):
        for j in range(NCH):
            gp.wait_ge(C["f_v"].h, fvG[13][j])
            if j >= 2:
                gp.wait_ge(C["f_out0" if j % 2 == 0 else "f_out1"].h,
                           16 * (j // 2))
            gp.drain()
            C["f_p"].inc(gp.tensor_tensor(Z.g_x2[j % 2][:], Z.f_x1[j % 3][:],
                                          Z.g_t[j % 2][:], ALU.add))
            gp.drain()
            C["f_out0" if j % 2 == 0 else "f_out1"].inc(gp.dma_start(
                out=Z.y_h[j * 128:(j + 1) * 128, :], in_=Z.g_x2[j % 2][:]), 16)

    @block.vector
    def _(v):
        st = Z.stat
        v.wait_ge(C["f0_pe"].h, PE_BETA)

        def v_f1(v, sc):
            b = 32 * (sc % 2)
            v.wait_ge(C["f_ao0" if sc % 2 == 0 else "f_ao1"].h, f_ao_at[sc])
            v.reduce_sum(st[:, b + 0:b + 1], Z.f_ao[sc % 2][:], axis=AX.X)
            v.drain()
            C["f_v"].inc(v.tensor_scalar_mul(Z.m2_sb[:, sc:sc + 1],
                                             st[:, b + 0:b + 1], 1.0 / D))
            v.wait_ge(C["f_s"].h, fsS[2][sc])
            v.tensor_scalar_mul(st[:, b + 2:b + 3], st[:, b + 1:b + 2], 1.0 / D)
            v.drain()
            v.tensor_mul(st[:, b + 3:b + 4], Z.m2_sb[:, sc:sc + 1],
                         Z.m2_sb[:, sc:sc + 1])
            v.drain()
            C["f_v"].inc(v.tensor_sub(st[:, b + 4:b + 5], st[:, b + 2:b + 3],
                                      st[:, b + 3:b + 4]))
            v.wait_ge(C["f_s"].h, fsS[3][sc])
            C["f_v"].inc(v.tensor_scalar(Z.f_t[0][:], Z.f_ao[sc % 2][:],
                                         Z.m2_sb[:, sc:sc + 1],
                                         Z.r2_sb[:, sc:sc + 1],
                                         ALU.subtract, ALU.mult))
            v.wait_ge(C["f_x0" if sc % 2 == 0 else "f_x1"].h, f_x_at[sc])
            if sc >= 3:
                v.wait_ge(C["f_p"].h, fp_P2[sc - 3])
            v.drain()
            C["f_v"].inc(v.scalar_tensor_tensor(
                Z.f_x1[sc % 3][:], Z.f_x[sc % 2][:], 1.0, Z.f_t[0][:],
                ALU.mult, ALU.add, accum_out=st[:, b + 5:b + 6]))
            v.drain()
            C["f_v"].inc(v.scalar_tensor_tensor(
                Z.f_u, Z.f_x1[sc % 3][:], 1.0, Z.f_x1[sc % 3][:],
                ALU.mult, ALU.mult, accum_out=st[:, b + 7:b + 8]))
            v.drain()
            C["f_v"].inc(v.tensor_scalar_mul(st[:, b + 6:b + 7],
                                             st[:, b + 5:b + 6], 1.0 / D))
            v.tensor_scalar_mul(st[:, b + 8:b + 9], st[:, b + 7:b + 8], 1.0 / D)
            v.drain()
            v.tensor_mul(st[:, b + 9:b + 10], st[:, b + 6:b + 7], st[:, b + 6:b + 7])
            v.drain()
            C["f_v"].inc(v.tensor_sub(st[:, b + 10:b + 11], st[:, b + 8:b + 9],
                                      st[:, b + 9:b + 10]))
            v.wait_ge(C["f_s"].h, fsS[5][sc])
            if sc >= 2:
                v.wait_ge(C["f_pe"].h, pe_T(sc - 2))
            C["f_v"].inc(v.tensor_scalar(Z.f_stg[sc % 2], Z.f_x1[sc % 3][:],
                                         st[:, b + 6:b + 7], st[:, b + 12:b + 13],
                                         ALU.subtract, ALU.mult))

        def v_gtail(v, j):
            b = 32 * (j % 2)
            grp = psA if j % 2 == 0 else psB
            v.wait_ge(C["f_s"].h, fsS[6][j])
            v.drain()
            C["f_v"].inc(v.tensor_scalar_mul(st[:, b + 15:b + 16],
                                             st[:, b + 13:b + 14],
                                             1.0 / D))
            v.wait_ge(C["f_s"].h, fsS[7][j])
            v.tensor_scalar_mul(st[:, b + 16:b + 17], st[:, b + 14:b + 15], 1.0 / D)
            v.drain()
            v.tensor_mul(st[:, b + 21:b + 22], st[:, b + 15:b + 16],
                         st[:, b + 15:b + 16])
            v.drain()
            C["f_v"].inc(v.tensor_sub(st[:, b + 17:b + 18], st[:, b + 16:b + 17],
                                      st[:, b + 21:b + 22]))
            v.wait_ge(C["f_s"].h, fsS[8][j])
            if j >= 2:
                v.wait_ge(C["f_p"].h, fp_P2[j - 2])
            C["f_v"].inc(v.tensor_scalar(Z.g_t[j % 2][:], grp[:],
                                         st[:, b + 15:b + 16],
                                         st[:, b + 19:b + 20],
                                         ALU.subtract, ALU.mult))

        def v_evac(v, sc):
            v.wait_ge(C["f_pe"].h, pe_T(sc))
            grp = psA if sc % 2 == 0 else psB
            C["f_v"].inc(v.tensor_copy(Z.xn2w[sc % 2][:, 0:1024],
                                       grp[:, 0:1024]))

        v_f1(v, 0)
        for sc in range(NCH):
            v_evac(v, sc)
            if sc + 1 < NCH:
                v_f1(v, sc + 1)
            if sc >= 1:
                v_gtail(v, sc - 1)
        v_gtail(v, NCH - 1)

    @block.tensor
    def _(t):
        for sc in range(NCH):
            t.wait_ge(C["f_v"].h, fvV[8][sc])
            if sc >= 2:
                t.wait_ge(C["f_v"].h, fvG[13][sc - 2])
            grp = psA if sc % 2 == 0 else psB
            for kc in range(16):
                mm = t.matmul(grp[:, kc * 128:(kc + 1) * 128],
                              Z.f_stg[sc % 2][:, kc * 128:(kc + 1) * 128],
                              Z.ident[:], start=True, stop=True)
            C["f_pe"].inc(mm)
            t.wait_ge(C["f_v"].h, fvV[9][sc])
            t.wait_ge(C["f_s"].h, fsS[10][sc])
            _emit_f3(Z, t, sc)

    _tail(block, C)


def _emit_f3(Z, t, sc):
    C = Z.C
    psA, psB = Z.psA, Z.psB
    grp = psA if sc % 2 == 0 else psB
    for k in range(NCH):
        lh = Z.xn2w[sc % 2][:, k * 128:(k + 1) * 128]
        for nt in range(4):
            mm = t.matmul(grp[:, nt * 512:(nt + 1) * 512], lh,
                          Z.qwt[:, k * 2048 + nt * 512:k * 2048 + (nt + 1) * 512],
                          start=(k == 0), stop=False)
    for nt in range(4):
        mm = t.matmul(grp[:, nt * 512:(nt + 1) * 512], Z.ones_bf[:],
                      Z.bt_row[:, nt * 512:(nt + 1) * 512],
                      start=False, stop=True)
    C["f_pe"].inc(mm)


def _tail(block, C, waits=None):
    @block.sync
    def _(sync):
        for name, ctr in C.items():
            if ctr.n > 0:
                sync.wait_ge(ctr.h, ctr.n)


_TABLES = None


def kernel(**inputs):
    global _TABLES
    if _TABLES is None:
        _TABLES = host_tables()
    cos_t, sinsgn_t, ident = _TABLES

    x = np.ascontiguousarray(inputs["L0_attn_hidden_states"])
    gamma = np.ascontiguousarray(
        inputs["L0_ffn_gamma"].reshape(16, 128).T).astype(np.float32)
    rows3 = np.concatenate([inputs["L0_ffn_ln_weight"],
                            inputs["L0_ffn_ln_bias"],
                            inputs["L0_ffn_beta"]]).reshape(1, 3 * D)
    rows3 = np.ascontiguousarray(rows3).astype(np.float32)

    base = {
        "wq": np.ascontiguousarray(inputs["L0_attn_Wq"]),
        "wk": np.ascontiguousarray(inputs["L0_attn_Wk"]),
        "wv": np.ascontiguousarray(inputs["L0_attn_Wv"]),
        "wphi": np.ascontiguousarray(inputs["L0_attn_Wphi"]),
        "wo": np.ascontiguousarray(inputs["L0_attn_Wo"]),
        "wffn": np.ascontiguousarray(inputs["L0_ffn_weight"]),
        "gamma": gamma,
        "rows3": rows3,
        "cos_t": np.ascontiguousarray(cos_t),
        "sinsgn_t": np.ascontiguousarray(sinsgn_t),
        "ident": np.ascontiguousarray(ident),
    }
    in_maps = [dict(base, x=np.ascontiguousarray(x[b])) for b in range(8)]

    nc = build_nc()
    res = run_bass_kernel_spmd(nc, in_maps, core_ids=list(range(8)))
    out = np.stack([res.results[b]["y"] for b in range(8)], axis=0)
    return out.astype(np.float32)
